# version 47
# baseline (speedup 1.0000x reference)
import numpy as np
import concourse.bass as bass
import concourse.mybir as mybir
from concourse.bass_utils import run_bass_kernel_spmd

F32 = mybir.dt.float32
BF16 = mybir.dt.bfloat16
I32 = mybir.dt.int32
AL = mybir.AluOpType
AF = mybir.ActivationFunctionType
X = mybir.AxisListType.X

ENGS = ("tensor", "vector", "scalar", "gpsimd", "sync")
D = 1024
N = 2048
TT = 256
L = 128
NIC = N // 128
NDC = D // 128


class Rec:

    def __init__(self):
        self.ops = {e: [] for e in ENGS}
        self.semv = {}
        self.lastw = {e: {} for e in ENGS}

    def do(self, eng, fn, waits=(), inc=None, by=1):
        w = []
        for t in waits:
            if t is None:
                continue
            s, v = t
            if v > 0 and self.lastw[eng].get(s, 0) < v:
                w.append((s, v))
                self.lastw[eng][s] = v
        tok = None
        if inc is not None:
            self.semv[inc] = self.semv.get(inc, 0) + by
            tok = (inc, self.semv[inc])
        self.ops[eng].append((fn, tuple(w), inc, by))
        return tok


def build(n_tiles=8):
    T = n_tiles * TT
    nc = bass.Bass("TRN2", target_bir_lowering=False)
    r = Rec()

    d_xfm = nc.dram_tensor("x_fm", [D, T], F32, kind="ExternalInput")
    d_xtm = nc.dram_tensor("x_tm", [T, D], F32, kind="ExternalInput")
    d_win = nc.dram_tensor("w_in", [2 * NIC, 128, NDC * 128], F32, kind="ExternalInput")
    d_wl = nc.dram_tensor("w_l", [NIC, 128, N], F32, kind="ExternalInput")
    d_wv = nc.dram_tensor("w_v", [NIC, 128, N], F32, kind="ExternalInput")
    d_wo = nc.dram_tensor("w_out", [2, NIC, 128, 512], F32, kind="ExternalInput")
    d_wf = nc.dram_tensor("w_ffn", [D, D], I32, kind="ExternalInput")
    d_bl = nc.dram_tensor("bl_fm", [128, NIC], F32, kind="ExternalInput")
    d_bv = nc.dram_tensor("bv_fm", [128, NIC], F32, kind="ExternalInput")
    d_om = nc.dram_tensor("om_fm", [128, NIC], F32, kind="ExternalInput")
    d_bf = nc.dram_tensor("b_ffn", [128, D], F32, kind="ExternalInput")
    d_sc = nc.dram_tensor("scal", [1, 1], F32, kind="ExternalInput")
    d_out = nc.dram_tensor("out", [T, D], F32, kind="ExternalOutput")
    d_wls = nc.dram_tensor("wls", [NIC, 128, 2 * N], BF16, kind="Internal")
    d_wvs = nc.dram_tensor("wvs", [NIC, 128, 2 * N], BF16, kind="Internal")
    d_wos = nc.dram_tensor("wos", [2, NIC // 2, 128, 2048], BF16,
                           kind="Internal")
    c_id = nc.inline_tensor(np.eye(128, dtype=np.float32), "c_ident")
    c_sl = nc.inline_tensor(np.tril(np.ones((128, 128), np.float32), -1), "c_sl")
    c_sl2 = nc.inline_tensor(np.tril(np.full((128, 128), -2.0, np.float32), -1),
                             "c_sl2")
    c_on = nc.inline_tensor(np.ones((1, 128), np.float32), "c_ones")

    sb = nc.alloc_sbuf_tensor
    XI = sb("XI", [128, NIC * TT], F32)
    ZS = sb("ZS", [128, NIC * TT], F32)
    VV_ = sb("VB", [128, NIC * TT], F32)
    LAM = sb("LAM", [128, NIC * TT], F32)
    GC = sb("GC", [128, NIC * TT], F32)
    KK = sb("KK", [128, NIC * TT], F32)
    XFM = sb("XFM", [128, 2 * NDC * TT], F32)
    XTM = sb("XTM", [128, (TT // 128) * D], F32)
    R1 = sb("R1", [128, (TT // 128) * D], F32)
    Y1 = sb("Y1", [128, (TT // 128) * D], F32)
    XNT = sb("XNT", [128, NDC * 128], BF16)
    WFF = sb("WFF", [128, NDC * D], BF16)
    WIN = sb("WIN", [128, 2 * NDC * 128], F32)
    WLV = sb("WLV", [128, 2 * N], F32)
    WOU = sb("WOU", [128, 4 * 512], F32)
    ZRO = sb("ZRO", [128, TT], F32)
    S0 = sb("S0", [128, 128], F32)
    PA = sb("PA", [128, 128], F32)
    PB = sb("PB", [128, 128], F32)
    STA = sb("STA", [128, 128], F32)
    STB = sb("STB", [128, 128], F32)
    GB = sb("GB", [128, 128], F32)
    SCG = sb("SCG", [128, 128], F32)
    IDE = sb("IDE", [128, 128], F32)
    CSL = sb("CSL", [128, 128], F32)
    CSL2 = sb("CSL2", [128, 128], F32)
    CON = sb("CON", [1, 128], F32)
    STATE = sb("STATE", [128, NIC], F32)
    BLc = sb("BLc", [128, NIC], F32)
    BVc = sb("BVc", [128, NIC], F32)
    WOM = sb("WOM", [128, NIC], F32)
    WOM2 = sb("WOM2", [128, NIC], F32)
    WOMN = sb("WOMN", [128, NIC], F32)
    STATE2 = sb("STATE2", [128, NIC], F32)
    SB0 = sb("SB0", [128, NIC], F32)
    XG = sb("XG", [128, 128], F32)
    XGT = sb("XGT", [128, 128], F32)
    S0B = sb("S0B", [128, 128], F32)
    K1B = sb("K1B", [128, 1], F32)
    DVN1 = sb("DVN1", [128, 1], F32)
    DVP1 = sb("DVP1", [128, 1], F32)
    RHO1 = sb("RHO1", [128, 1], F32)
    UU1 = sb("UU1", [128, 1], F32)
    U2A = sb("U2A", [128, 1], F32)
    U2B = sb("U2B", [128, 1], F32)
    KSC = sb("KSC", [128, 4 * TT], F32)
    VVd = sb("VVd", [128, 1], F32)
    DVN = sb("DVN", [128, 1], F32)
    DVP = sb("DVP", [128, 1], F32)
    RHO = sb("RHO", [128, 1], F32)
    UU = sb("UU", [128, 1], F32)
    GROW = sb("GROW", [1, 128], F32)
    A1 = sb("A1", [1, 2], F32)
    SB_ = sb("SB_", [128, 1], F32)
    RSC = sb("RSC", [128, 1], F32)
    STT = sb("STT", [128, 20], F32)
    BFN = sb("BFN", [128, D], F32)
    FFS = sb("FFS", [128, D], F32)
    FFSI = FFS.bitcast(I32)

    XIb = XI.bitcast(BF16)
    WOUb_h = WOU.bitcast(BF16)
    LAMb = LAM.bitcast(BF16)

    pb = [nc.alloc_psum_tensor("pb%d" % i, [128, 512], F32) for i in range(8)]

    def fm(buf, ic, a, b):
        return buf[:, ic * TT + a: ic * TT + b]

    def fm3(buf, a, b):
        return buf[:].rearrange("p (c t) -> p c t", c=NIC)[:, :, a:b]

    drain = {}
    fill = {}

    def mm(bank_ap, lhsT, rhs, start, stop, waits=(), inc=None):
        def f(e, o=bank_ap, l=lhsT, rr=rhs, st=start, sp=stop):
            return e.matmul(o, l, rr, start=st, stop=sp)
        return r.do("tensor", f, waits=waits, inc=inc)

    def dma(eng, dst, src, sem, waits=()):
        def f(e, d=dst, s=src):
            return e.dma_start(d, s)
        return r.do(eng, f, waits=waits, inc=sem, by=16)

    dma("sync", IDE[:, :], c_id[:, :], "dcon")
    dma("sync", CSL[:, :], c_sl[:, :], "dcon")
    dma("sync", CSL2[:, :], c_sl2[:, :], "dcon")
    dma("sync", CON[:, :], c_on[:, :], "dcon")
    dma("sync", BLc[:, :], d_bl[:, :], "dcon")
    dma("sync", BVc[:, :], d_bv[:, :], "dcon")
    dma("sync", WOM[:, :], d_om[:, :], "dcon")
    dma("sync", BFN[:, :], d_bf[:, :], "dcon")
    t_pro = dma("sync", A1[0:1, 0:1], d_sc[:, :], "dcon")
    t_con = t_sc = t_pro
    stg = WIN.bitcast(I32)
    t_cv = None
    for k in range(NDC):
        sl = k % 2
        t_l = dma("sync", stg[:, sl * 1024:(sl + 1) * 1024],
                  d_wf[k * 128:(k + 1) * 128, :], "dff",
                  waits=(t_cv,))
        t_cv = r.do("vector", lambda e, a=stg[:, sl * 1024:(sl + 1) * 1024],
                    o=WFF[:, k * D:(k + 1) * D]:
                    e.tensor_scalar_add(o, a, -1.0),
                    waits=(t_l,), inc="dv")
    t_wffn = t_cv
    t1 = r.do("scalar", lambda e: e.activation(WOM2[:, :], WOM[:, :], AF.Exp),
              waits=(t_pro,), inc="ac")
    t1 = r.do("scalar", lambda e: e.activation(WOM2[:, :], WOM2[:, :], AF.Ln, bias=1.0),
              waits=(t1,), inc="ac")
    t_wom = r.do("vector", lambda e: e.tensor_scalar_mul(WOM2[:, :], WOM2[:, :], -8.0),
                 waits=(t1,), inc="dv")
    t_wom = r.do("vector", lambda e: e.tensor_scalar_mul(WOMN[:, :], WOM2[:, :], -1.0),
                 waits=(t_wom,), inc="dv")
    t_zro = r.do("vector", lambda e: e.memset(ZRO[:, :], 0.0), inc="dv")
    t_s1 = r.do("vector", lambda e: e.tensor_scalar_max(A1[0:1, 0:1], A1[0:1, 0:1], 1e-5),
                waits=(t_sc,), inc="dv")
    t_s2 = mm(pb[4][:, 0:1], CON[0:1, 0:128], A1[0:1, 0:1], True, True,
              waits=(t_s1, t_con), inc="pe")
    t_s3 = r.do("vector", lambda e: e.tensor_copy(SB_[:, :], pb[4][:, 0:1]),
                waits=(t_s2,), inc="dv")
    drain[4] = t_s3
    t_rs = r.do("vector", lambda e: e.reciprocal(RSC[:, :], SB_[:, :]),
                waits=(t_s3,), inc="dv")

    win_free = [None, None]
    wlv_free = [None, None]
    wou_free = [None] * 4
    xfm_free = [None, None]
    xtm_free = [None, None]
    nout = 0
    t_y2 = [None] * (TT // 128)
    t_state = None
    t_state_pe = None
    t_hprev = ()
    ksl = [None, None, None, None]
    tjsp = [None, None]
    tjout2 = [None, None]
    tjmm2 = [None, None]
    split_done = {"stage": (), "wl": (), "wv": (), "wo": ()}
    t_jsplit = None
    t_jmm = None
    t_jout = None
    ffn_tok = [None] * (TT // 128)
    t_xnt_free = None
    t_gprev = None

    def emit_front_loads(tile):
        t0 = tile * TT
        par = tile % 2
        xfo = par * NDC * TT
        t_xfm = None
        for dc in range(NDC):
            t_xfm = dma("gpsimd", XFM[:, xfo + dc * TT:xfo + (dc + 1) * TT],
                        d_xfm[dc * 128:(dc + 1) * 128, t0:t0 + TT],
                        "dxf%d" % par, waits=(xfm_free[par],))
        return dict(tile=tile, par=par, xfo=xfo, xto=0, t_xfm=t_xfm,
                    t_xi=[None] * NIC, t_zs=[None] * NIC, t_hi15=None,
                    t_s1done=None, ec=0, zgate=([] if tile == 0 else None))

    def emit_s1_group(st):
        ec = st["ec"]
        st["ec"] = ec + 1
        sl = ec % 2
        wslot = WIN[:, sl * 1024:(sl + 1) * 1024]
        t_w = dma("sync", wslot[:, 0:256], d_win[ec][:, 0:256], "dw%d" % sl,
                  waits=(win_free[sl], t_wffn))
        t_wa = dma("sync", wslot[:, 256:512], d_win[ec][:, 256:512],
                   "dw%d" % (sl + 2), waits=(win_free[sl], t_wffn))
        t_wb = dma("sync", wslot[:, 512:768], d_win[ec][:, 512:768],
                   "dw%d" % (sl + 4), waits=(win_free[sl], t_wffn))
        t_wc = dma("sync", wslot[:, 768:1024], d_win[ec][:, 768:1024],
                   "dw%d" % (sl + 6), waits=(win_free[sl], t_wffn))
        bank = ec % 4
        qg = {0: None, 2: t_wa, 4: t_wb, 6: t_wc}
        for dc in range(NDC):
            tok = mm(pb[bank][:, 0:TT], wslot[:, dc * 128:(dc + 1) * 128],
                     XFM[:, st["xfo"] + dc * TT:st["xfo"] + (dc + 1) * TT],
                     dc == 0, dc == NDC - 1,
                     waits=(t_w, st["t_xfm"], drain.get(bank)) if dc == 0
                     else (qg.get(dc),) if dc in qg else (),
                     inc="pe" if dc == NDC - 1 else None)
        fill[bank] = tok
        win_free[sl] = tok
        st["t_s1done"] = tok
        if ec < NIC:
            hi = XIb[:, ec * TT:(ec + 1) * TT]
            lo = XIb[:, NIC * TT + ec * TT: NIC * TT + (ec + 1) * TT]
            t_hi = r.do("scalar", lambda e, o=hi, i=pb[bank][:, 0:TT]:
                        e.activation(o, i, AF.Copy), waits=(tok,), inc="ac")
            st["t_hi15"] = t_hi
            tev = r.do("vector", lambda e, o=lo, i=pb[bank][:, 0:TT], h=hi:
                       e.tensor_tensor(o, i, h, AL.subtract),
                       waits=(t_hi,), inc="dv")
            st["t_xi"][ec] = tev
        else:
            dst = fm(ZS, ec - NIC, 0, TT)
            tev = r.do("scalar", lambda e, o=dst, i=pb[bank][:, 0:TT]:
                       e.activation(o, i, AF.Silu),
                       waits=(tok,) + tuple(st["zgate"] or ()), inc="ac")
            st["t_zs"][ec - NIC] = tev
        drain[bank] = tev

    s1 = emit_front_loads(0)
    while s1["ec"] < 2 * NIC:
        emit_s1_group(s1)
    for tile in range(n_tiles):
        t0 = tile * TT
        par = s1["par"]
        xfo = s1["xfo"]
        xto = s1["xto"]
        t_xi = s1["t_xi"]
        t_zs = s1["t_zs"]
        t_hi15 = s1["t_hi15"]
        t_xfm = s1["t_xfm"]
        t_s1done = s1["t_s1done"]
        for tc in range(TT // 128):
            t_xtm = dma("gpsimd", XTM[:, xto + tc * D:xto + (tc + 1) * D],
                        d_xtm[t0 + tc * 128:t0 + (tc + 1) * 128, :], "dxt",
                        waits=(xtm_free[0],))
        t_lam = [None] * NIC
        t_sig = [None] * NIC
        for jc in range(NIC):
            sl = jc % 2
            if tile == 0:
                p = jc % 2
                if p == 0:
                    slotf = WLV[:, 0:N]
                    wslotb = WLV.bitcast(BF16)[:, 2 * N:4 * N]
                else:
                    slotf = R1[:, 0:N]
                    wslotb = Y1.bitcast(BF16)[:, 0:2 * N]
                t_l = dma("sync", slotf, d_wl[jc], "dl%d" % p,
                          waits=(tjsp[p],))
                jeng, jsem = "vector", "dv"
                t_js = r.do(jeng, lambda e, o=wslotb[:, 0:N], i=slotf:
                            e.tensor_copy(o, i),
                            waits=(t_l, tjmm2[p], tjout2[p], tjsp[p]),
                            inc=jsem)
                tjsp[p] = r.do(jeng, lambda e, o=wslotb[:, N:2 * N], i=slotf,
                               h=wslotb[:, 0:N]:
                               e.tensor_tensor(o, i, h, AL.subtract),
                               waits=(t_js,), inc=jsem)
                dma("sync", d_wls[jc][:, 0:N], wslotb[:, 0:N], "dgl%d" % p,
                    waits=(tjsp[p],))
                tjout2[p] = dma("sync", d_wls[jc][:, N:2 * N],
                                wslotb[:, N:2 * N], "dgl%d" % p, waits=())
                t_w = tjsp[p]
                split_done["wl"] = (tjout2[0], tjout2[1])
            else:
                wslotb = WLV.bitcast(BF16)[:, sl * 2 * N:(sl + 1) * 2 * N]
                t_w = dma("sync", wslotb[:, 0:N // 2], d_wls[jc][:, 0:N // 2],
                          "dl%d" % sl,
                          waits=(wlv_free[sl],) + split_done["wl"])
                t_wq = dma("sync", wslotb[:, N // 2:N],
                           d_wls[jc][:, N // 2:N], "dl%d" % (sl + 4),
                           waits=(wlv_free[sl],) + split_done["wl"])
                t_w2 = dma("sync", wslotb[:, N:2 * N], d_wls[jc][:, N:2 * N],
                           "dl%d" % (sl + 2),
                           waits=(wlv_free[sl],) + split_done["wl"])
            if tile == 0:
                t_w2 = t_w
                t_wq = t_w
            bank = (jc // 2) % 4
            pcol = (jc % 2) * TT
            for ic in range(NIC):
                wh = wslotb[:, ic * 128:(ic + 1) * 128]
                xih = XIb[:, ic * TT:(ic + 1) * TT]
                w0 = (t_w, t_hi15, drain.get(bank)) if ic == 0 and jc % 2 == 0 \
                    else (t_w,) if ic == 0 else (t_wq,) if ic == 8 else ()
                mm(pb[bank][:, pcol:pcol + TT], wh, xih,
                   ic == 0 and jc % 2 == 0, False, waits=w0)
            for ic in range(NIC):
                wh = wslotb[:, ic * 128:(ic + 1) * 128]
                xil = XIb[:, NIC * TT + ic * TT: NIC * TT + (ic + 1) * TT]
                mm(pb[bank][:, pcol:pcol + TT], wh, xil, False, False,
                   waits=(t_xi[NIC - 1],) if ic == 0 else ())
            for ic in range(NIC):
                wl_ = wslotb[:, N + ic * 128: N + (ic + 1) * 128]
                xih = XIb[:, ic * TT:(ic + 1) * TT]
                tok = mm(pb[bank][:, pcol:pcol + TT], wl_, xih,
                         False, ic == NIC - 1 and jc % 2 == 1,
                         waits=(t_w2,) if ic == 0 else (),
                         inc="pe" if ic == NIC - 1 else None)
            fill[bank] = tok
            wlv_free[sl] = tok
            t_jmm = tok
            tjmm2[jc % 2] = tok
            if jc % 2 == 1:
                for j2 in (jc - 1, jc):
                    tev = r.do("scalar", lambda e, o=fm(KK, j2, 0, TT),
                               i=pb[bank][:, (j2 % 2) * TT:(j2 % 2) * TT + TT],
                               b=BLc[:, j2:j2 + 1]:
                               e.activation(o, i, AF.Sigmoid, bias=b),
                               waits=(tok,) + tuple(t_hprev), inc="ac")
                    t_sig[j2] = tev
                drain[bank] = tev
        t_lami = [None] * NIC
        for jc in range(NIC):
            t_lam[jc] = r.do("scalar", lambda e, o=fm(LAM, jc, 0, TT),
                             i=fm(KK, jc, 0, TT), s=WOM2[:, jc:jc + 1]:
                             e.activation(o, i, AF.Exp, scale=s),
                             waits=(t_wom, t_sig[jc], t_gprev), inc="ac")
        for jc in range(NIC):
            t_lami[jc] = r.do("scalar", lambda e, o=fm(KK, jc, 0, TT),
                              s=WOMN[:, jc:jc + 1]:
                              e.activation(o, o, AF.Exp, scale=s),
                              waits=(t_lam[jc],), inc="ac")

        t_gc = [None] * NIC
        t_rr = [None] * NIC
        tE = None

        def emit_gauge_ic(ic):
            nonlocal tE
            t_ms = None
            if tile == 0:
                t_ms = r.do("vector", lambda e, o=fm(LAM, ic, 0, 1):
                            e.memset(o, 1.0), waits=(t_lam[ic], tE), inc="dv")
                t_ms = r.do("vector", lambda e, o=fm(KK, ic, 0, 1):
                            e.memset(o, 1.0), waits=(t_lami[ic], t_ms),
                            inc="dv")
            tE = t_gc[ic] = r.do(
                "vector", lambda e, o=fm(GC, ic, 0, TT), d0=fm(LAM, ic, 0, TT):
                e.tensor_tensor_scan(o, d0, ZRO[:, 0:TT], 1.0, AL.mult, AL.add),
                waits=(t_lam[ic], t_zro, t_ms, tE, t_gprev), inc="dv")
            tE = t_rr[ic] = r.do(
                "vector", lambda e, o=fm(KK, ic, 0, TT):
                e.tensor_tensor_scan(o, o, ZRO[:, 0:TT], 1.0, AL.mult, AL.add),
                waits=(t_lami[ic], tE), inc="dv")
            tE = r.do("vector", lambda e, o=fm(LAM, ic, 1, TT),
                      i0=fm(KK, ic, 1, TT), i1=fm(KK, ic, 0, TT - 1):
                      e.tensor_sub(o, i0, i1), waits=(tE,), inc="dv")
            tE = r.do("vector", lambda e, o=fm(LAM, ic, 0, 1),
                      i=fm(KK, ic, 0, 1): e.tensor_scalar_add(o, i, -1.0),
                      waits=(tE,), inc="dv")
            rec = XFM[:, xfo + (ic % NDC) * TT:xfo + (ic % NDC) * TT + TT]
            tE = r.do("vector", lambda e, o=rec,
                      i0=XIb[:, ic * TT:(ic + 1) * TT],
                      i1=XIb[:, NIC * TT + ic * TT:NIC * TT + (ic + 1) * TT]:
                      e.tensor_add(o, i0, i1),
                      waits=(tE, t_s1done, t_xi[ic]), inc="dv")
            tE = r.do("vector", lambda e, o=fm(LAM, ic, 0, TT), i1=rec:
                      e.tensor_mul(o, o, i1), waits=(tE,), inc="dv")

        if tile != 0:
            for ic in range(NIC):
                emit_gauge_ic(ic)
        t_bh = tE

        t_rb = None
        if tile != 0:
            t_rb = r.do("vector", lambda e: e.reduce_sum(
                SB0[:, :].unsqueeze(2), fm3(LAM, 0, L), X),
                waits=(t_bh, t_state_pe), inc="dv")
            t_rb = r.do("vector", lambda e: e.tensor_add(
                SB0[:, :], SB0[:, :], STATE[:, :]),
                waits=(t_rb, t_state), inc="dv")

        t_v = [None] * NIC
        t_ka_last = None
        kat = {}

        def emit_ka_ic(ic):
            nonlocal t_ka_last
            sl_ = ic % 4
            ks = KSC[:, sl_ * TT:(sl_ + 1) * TT]
            tk = r.do("vector", lambda e, o=ks[:, 0:1], i=fm(VV_, ic, 0, 1):
                      e.tensor_copy(o, i),
                      waits=(t_v[ic], t_gc[ic], ksl[sl_]), inc="dv")
            tk = r.do("vector", lambda e, o=ks[:, 1:TT],
                      i0=fm(GC, ic, 0, TT - 1), i1=fm(VV_, ic, 1, TT):
                      e.tensor_mul(o, i0, i1), waits=(tk,), inc="dv")
            ta = r.do("vector", lambda e, o=fm(VV_, ic, 1, TT),
                      i1=fm(KK, ic, 0, TT - 1): e.tensor_mul(o, o, i1),
                      waits=(tk, t_rr[ic]), inc="dv")
            kat[ic] = ta
            t_ka_last = ta

        def emit_gram_ic(ic):
            sl_ = ic % 4
            ks = KSC[:, sl_ * TT:(sl_ + 1) * TT]
            ta = kat[ic]
            w0 = (ta, t_state, t_rb, drain.get(4), drain.get(5),
                  drain.get(6), drain.get(7)) if ic == 0 else (ta,)
            mm(pb[4][:, 0:128], ks[:, 0:L], fm(VV_, ic, 0, L),
               ic == 0, ic == NIC - 1, waits=w0)
            mm(pb[5][:, 0:1], ks[:, 0:L], STATE[:, ic:ic + 1],
               ic == 0, False)
            mm(pb[5][:, 1:129], ks[:, 0:L], fm(LAM, ic, 0, L),
               False, False)
            mm(pb[5][:, 129:130], ks[:, L:TT], SB0[:, ic:ic + 1],
               False, ic == NIC - 1)
            mm(pb[6][:, 0:128], ks[:, L:TT], fm(VV_, ic, L, TT),
               ic == 0, ic == NIC - 1)
            mm(pb[7][:, 0:128], ks[:, L:TT], fm(LAM, ic, L, TT),
               ic == 0, False)
            tok = mm(pb[7][:, 128:256], ks[:, L:TT],
                     fm(VV_, ic, 0, L), False, ic == NIC - 1, inc="pe")
            ksl[sl_] = tok
            return tok

        for jc in range(NIC):
            sl = jc % 2
            if tile == 0:
                p = jc % 2
                if p == 0:
                    slotf = WLV[:, 0:N]
                    wslotb = WLV.bitcast(BF16)[:, 2 * N:4 * N]
                else:
                    slotf = R1[:, 0:N]
                    wslotb = Y1.bitcast(BF16)[:, 0:2 * N]
                t_l = dma("sync", slotf, d_wv[jc], "dl%d" % p,
                          waits=(tjsp[p],))
                jeng, jsem = "vector", "dv"
                t_js = r.do(jeng, lambda e, o=wslotb[:, 0:N], i=slotf:
                            e.tensor_copy(o, i),
                            waits=(t_l, tjmm2[p], tjout2[p], tjsp[p]),
                            inc=jsem)
                tjsp[p] = r.do(jeng, lambda e, o=wslotb[:, N:2 * N], i=slotf,
                               h=wslotb[:, 0:N]:
                               e.tensor_tensor(o, i, h, AL.subtract),
                               waits=(t_js,), inc=jsem)
                dma("sync", d_wvs[jc][:, 0:N], wslotb[:, 0:N], "dgv%d" % p,
                    waits=(tjsp[p],))
                tjout2[p] = dma("sync", d_wvs[jc][:, N:2 * N],
                                wslotb[:, N:2 * N], "dgv%d" % p, waits=())
                t_w = tjsp[p]
                split_done["wv"] = (tjout2[0], tjout2[1])
            else:
                wslotb = WLV.bitcast(BF16)[:, sl * 2 * N:(sl + 1) * 2 * N]
                t_w = dma("sync", wslotb[:, 0:N // 2], d_wvs[jc][:, 0:N // 2],
                          "dl%d" % sl,
                          waits=(wlv_free[sl],) + split_done["wv"])
                t_wq = dma("sync", wslotb[:, N // 2:N],
                           d_wvs[jc][:, N // 2:N], "dl%d" % (sl + 4),
                           waits=(wlv_free[sl],) + split_done["wv"])
                t_w2 = dma("sync", wslotb[:, N:2 * N], d_wvs[jc][:, N:2 * N],
                           "dl%d" % (sl + 2),
                           waits=(wlv_free[sl],) + split_done["wv"])
            if tile == 0:
                t_w2 = t_w
                t_wq = t_w
            bank = (jc // 2) % 4
            pcol = (jc % 2) * TT
            for ic in range(NIC):
                wh = wslotb[:, ic * 128:(ic + 1) * 128]
                xih = XIb[:, ic * TT:(ic + 1) * TT]
                w0 = (t_w, drain.get(bank)) if ic == 0 and jc % 2 == 0 \
                    else (t_w,) if ic == 0 else (t_wq,) if ic == 8 else ()
                mm(pb[bank][:, pcol:pcol + TT], wh, xih,
                   ic == 0 and jc % 2 == 0, False, waits=w0)
            for ic in range(NIC):
                wh = wslotb[:, ic * 128:(ic + 1) * 128]
                xil = XIb[:, NIC * TT + ic * TT: NIC * TT + (ic + 1) * TT]
                mm(pb[bank][:, pcol:pcol + TT], wh, xil, False, False,
                   waits=(t_xi[NIC - 1],) if ic == 0 else ())
            for ic in range(NIC):
                wl_ = wslotb[:, N + ic * 128: N + (ic + 1) * 128]
                xih = XIb[:, ic * TT:(ic + 1) * TT]
                tok = mm(pb[bank][:, pcol:pcol + TT], wl_, xih,
                         False, ic == NIC - 1 and jc % 2 == 1,
                         waits=(t_w2,) if ic == 0 else (),
                         inc="pe" if ic == NIC - 1 else None)
            fill[bank] = tok
            wlv_free[sl] = tok
            t_jmm = tok
            tjmm2[jc % 2] = tok
            if tile == 0:
                emit_gauge_ic(jc)
            if jc % 2 == 1:
                for j2 in (jc - 1, jc):
                    tev = r.do("scalar", lambda e, o=fm(VV_, j2, 0, TT),
                               i=pb[bank][:, (j2 % 2) * TT:(j2 % 2) * TT + TT],
                               b=BVc[:, j2:j2 + 1]:
                               e.activation(o, i, AF.Identity, bias=b),
                               waits=(tok,), inc="ac")
                    t_v[j2] = tev
                drain[bank] = tev
        if tile == 0:
            t_bh = tE
        if tile == 0:
            split_done["stage"] = split_done["stage"] + (tjsp[1], tjout2[1])
            split_done["wl"] = split_done["wl"] + split_done["wv"] + \
                (tjsp[0], tjsp[1])
        if tile != 0:
            for ic in range(NIC):
                emit_ka_ic(ic)
                t_gram = emit_gram_ic(ic)

        t_b = None
        if tile == 0:
            xibh = XIb[:, 0:NIC * TT].rearrange("p (c t) -> p c t", c=NIC)
            xibl = XIb[:, NIC * TT:2 * NIC * TT].rearrange("p (c t) -> p c t",
                                                           c=NIC)
            t_rc = r.do("vector", lambda e, i0=xibh[:, :, 0:1].squeeze(2),
                        i1=xibl[:, :, 0:1].squeeze(2):
                        e.tensor_add(STATE2[:, :], i0, i1),
                        waits=(t_xi[NIC - 1],), inc="dv")
            for ic in range(NIC):
                w0 = (t_v[NIC - 1], t_rc, drain.get(4), t_con) if ic == 0 else ()
                mm(pb[4][0:1, 0:1], fm(VV_, ic, 0, 1), STATE2[:, ic:ic + 1],
                   ic == 0, False, waits=w0)
                tok = mm(pb[4][0:1, 1:2], fm(VV_, ic, 0, 1), fm(VV_, ic, 0, 1),
                         False, ic == NIC - 1,
                         inc="pe" if ic == NIC - 1 else None)
            fill[4] = tok
            t_a = r.do("vector", lambda e: e.reciprocal(A1[0:1, 0:1], pb[4][0:1, 1:2]),
                       waits=(tok,), inc="dv")
            t_a = r.do("vector", lambda e: e.tensor_mul(A1[0:1, 0:1], A1[0:1, 0:1],
                                                        pb[4][0:1, 0:1]),
                       waits=(t_a,), inc="dv")
            t_a = r.do("vector", lambda e: e.tensor_scalar_mul(A1[0:1, 0:1],
                                                               A1[0:1, 0:1], -2.0),
                       waits=(t_a,), inc="dv")
            drain[4] = t_a
            t_b = mm(pb[5][:, 0:1], CON[0:1, 0:128], A1[0:1, 0:1], True, True,
                     waits=(t_a, drain.get(5)), inc="pe")
            fill[5] = t_b
            t_ab = r.do("vector", lambda e: e.tensor_copy(GB[:, 0:1], pb[5][:, 0:1]),
                        waits=(t_b,), inc="dv")
            drain[5] = t_ab
            t_state = r.do("vector", lambda e: e.scalar_tensor_tensor(
                STATE[:, :], fm3(VV_, 0, 1).squeeze(2), GB[:, 0:1],
                STATE2[:, :], AL.mult, AL.add),
                waits=(t_ab, t_bh, t_rc), inc="dv")

        if tile == 0:
            t_rb = r.do("vector", lambda e: e.reduce_sum(
                SB0[:, :].unsqueeze(2), fm3(LAM, 0, L), X),
                waits=(t_bh,), inc="dv")
            t_rb = r.do("vector", lambda e: e.tensor_add(
                SB0[:, :], SB0[:, :], STATE[:, :]),
                waits=(t_rb, t_state), inc="dv")
            for ic in range(NIC):
                emit_ka_ic(ic)
                t_gram = emit_gram_ic(ic)

        t_g2c = [None] * (TT // L)
        s1n = emit_front_loads(tile + 1) if tile + 1 < n_tiles else None

        def pump(n=1):
            for _ in range(n):
                if s1n is None or s1n["ec"] >= 2 * NIC:
                    return
                if s1n["ec"] >= NIC and s1n["zgate"] is None:
                    return
                emit_s1_group(s1n)

        t_state_pe = t_gram
        fill[4] = fill[5] = fill[6] = fill[7] = t_gram
        t_vv = r.do("vector", lambda e: e.tensor_mul(
            SCG[:, :], pb[4][:, 0:128], IDE[:, :]), waits=(t_gram,), inc="dv")
        t_vv = r.do("vector", lambda e: e.reduce_sum(VVd[:, :], SCG[:, :], X),
                    waits=(t_vv,), inc="dv")
        t_dvn = r.do("vector", lambda e: e.reciprocal(DVN[:, :], VVd[:, :]),
                     waits=(t_vv,), inc="dv")
        t_S = r.do("vector", lambda e: e.scalar_tensor_tensor(
            S0[:, :], pb[4][:, 0:128], DVN[:, :], CSL2[:, :], AL.mult, AL.mult),
            waits=(t_dvn,), inc="dv")
        drain[4] = t_S
        t_vv1 = r.do("vector", lambda e: e.tensor_mul(
            SCG[:, :], pb[6][:, 0:128], IDE[:, :]), waits=(t_gram, t_vv),
            inc="dv")
        t_vv1 = r.do("vector", lambda e: e.reduce_sum(VVd[:, :], SCG[:, :], X),
                     waits=(t_vv1,), inc="dv")
        t_dvn1 = r.do("vector", lambda e: e.reciprocal(DVN1[:, :], VVd[:, :]),
                      waits=(t_vv1,), inc="dv")
        t_S1 = r.do("vector", lambda e: e.scalar_tensor_tensor(
            S0B[:, :], pb[6][:, 0:128], DVN1[:, :], CSL2[:, :],
            AL.mult, AL.mult), waits=(t_dvn1,), inc="dv")
        drain[6] = t_S1
        t_dvp = r.do("vector", lambda e: e.tensor_scalar_mul(DVP[:, :],
                                                             DVN[:, :], 2.0),
                     waits=(t_dvn,), inc="dv")
        t_dvp1 = r.do("vector", lambda e: e.tensor_scalar_mul(DVP1[:, :],
                                                              DVN1[:, :], 2.0),
                      waits=(t_dvn1,), inc="dv")
        t_rho = r.do("vector", lambda e: e.tensor_mul(
            SCG[:, :], pb[5][:, 1:129], CSL[:, :]), waits=(t_gram, t_vv1),
            inc="dv")
        t_rho = r.do("vector", lambda e: e.reduce_sum(RHO[:, :], SCG[:, :], X),
                     waits=(t_rho,), inc="dv")
        t_u0 = r.do("vector", lambda e: e.scalar_tensor_tensor(
            UU[:, :], pb[5][:, 0:1], RHO[:, :], DVP[:, :], AL.add, AL.mult),
            waits=(t_rho, t_dvp), inc="dv")
        drain[5] = t_u0
        t_rho1 = r.do("vector", lambda e: e.tensor_mul(
            SCG[:, :], pb[7][:, 0:128],
            CSL[:, :]), waits=(t_gram, t_rho), inc="dv")
        t_rho1 = r.do("vector", lambda e: e.reduce_sum(RHO1[:, :], SCG[:, :], X),
                      waits=(t_rho1,), inc="dv")
        drain[7] = t_rho1
        t_k1 = r.do("scalar", lambda e: e.activation(
            K1B[:, :], pb[5][:, 129:130], AF.Identity, bias=RHO1[:, :]),
            waits=(t_gram, t_rho1), inc="ac")
        t_xg = r.do("scalar", lambda e: e.activation(
            XG[:, :], pb[7][:, 128:256], AF.Copy),
            waits=(t_gram, t_k1), inc="ac")
        drain[7] = t_xg
        drain[4] = t_S
        t_xgt = r.do("tensor", lambda e: e.transpose(pb[5][:, 0:128], XG[:, :],
                                                     IDE[:, :]),
                     waits=(t_xg, t_u0, t_k1), inc="pe")
        fill[5] = t_xgt
        t_xgt = r.do("scalar", lambda e: e.activation(
            XGT[:, :], pb[5][:, 0:128], AF.Copy), waits=(t_xgt,), inc="ac")
        drain[5] = t_xgt
        drain[6] = t_S1
        pump(7)

        def chain(uu, u2, sbuf_S, t_u, t_Ssrc, dvp_unused):
            cs = None
            pcur = sbuf_S
            prev_p = t_Ssrc
            prev_s = None
            t_tp = r.do("tensor", lambda e, i=sbuf_S: e.transpose(
                pb[7][:, 0:128], i[:, :], IDE[:, :]),
                waits=(t_Ssrc, drain.get(7)), inc="pe")
            fill[7] = t_tp
            t_st_ = r.do("scalar", lambda e: e.activation(
                STA[:, :], pb[7][:, 0:128], AF.Copy),
                waits=(t_tp,), inc="ac")
            drain[7] = t_st_
            cs = STA
            prev_s = t_st_
            t_g = t_u
            cur, nxt = uu, u2
            for k in range(7):
                t_mv = mm(pb[5][:, 0:1], cs[:, :], cur[:, :], True, True,
                          waits=(prev_s, t_g, drain.get(5)), inc="pe")
                fill[5] = t_mv
                t_g = r.do("scalar", lambda e, p=pb[5][:, 0:1], o=nxt, b=cur:
                           e.activation(o[:, :], p, AF.Identity, bias=b[:, :]),
                           waits=(t_mv, t_g), inc="ac")
                drain[5] = t_g
                cur, nxt = nxt, cur
                pump(1)
                if k == 6:
                    break
                nxt_p = PB if pcur in (sbuf_S, PA) else PA
                nxt_s = STB if cs is STA else STA
                t_sq = mm(pb[4][:, 0:128], cs[:, :], pcur[:, :], True, True,
                          waits=(prev_p, prev_s, drain.get(4)), inc="pe")
                fill[4] = t_sq
                t_ep = r.do("scalar", lambda e, o=nxt_p: e.activation(
                    o[:, :], pb[4][:, 0:128], AF.Copy),
                    waits=(t_sq,), inc="ac")
                drain[4] = t_ep
                t_tp = r.do("tensor", lambda e, i=nxt_p: e.transpose(
                    pb[7][:, 0:128], i[:, :], IDE[:, :]),
                    waits=(t_ep, drain.get(7)), inc="pe")
                fill[7] = t_tp
                t_es = r.do("scalar", lambda e, o=nxt_s: e.activation(
                    o[:, :], pb[7][:, 0:128], AF.Copy),
                    waits=(t_tp,), inc="ac")
                drain[7] = t_es
                pcur, cs = nxt_p, nxt_s
                prev_p, prev_s = t_ep, t_es
            return t_g, cur

        def gamma_bcast(uu, t_g, bank, extra=()):
            t_tg = r.do("tensor", lambda e, u=uu: e.transpose(
                pb[7][0:1, 0:128], u[:, :], IDE[:, :]),
                waits=(t_g, drain.get(7)), inc="pe")
            fill[7] = t_tg
            t_gr = r.do("scalar", lambda e: e.activation(
                GROW[:, :], pb[7][0:1, 0:128], AF.Copy),
                waits=(t_tg,), inc="ac")
            drain[7] = t_gr
            t_gb = mm(pb[bank][:, 0:128], CON[0:1, 0:128], GROW[0:1, 0:128],
                      True, True, waits=(t_gr, drain.get(bank)), inc="pe")
            fill[bank] = t_gb
            t_gbe = r.do("scalar", lambda e: e.activation(
                GB[:, :], pb[bank][:, 0:128], AF.Copy),
                waits=(t_gb,) + tuple(extra), inc="ac")
            drain[bank] = t_gbe
            return t_gbe

        x3v = XFM[:, xfo:xfo + NIC * L].rearrange("p (c t) -> p c t", c=NIC)

        SPL = 12

        def dhg(c, t_gbe, sti, t_sti, t_x3free=()):
            a, b = c * L, (c + 1) * L

            def c3(buf, lo, hi):
                return buf[:].rearrange("p (c t) -> p c t",
                                        c=NIC)[:, lo:hi, a:b]

            lamb3 = LAMb[:].rearrange("p (c t) -> p c t", c=NIC)
            tks = [None, None]
            for hf, (eng, sem, lo, hi) in enumerate(
                    (("vector", "dv", 0, SPL), ("gpsimd", "po", SPL, NIC))):
                gbb = GB[:].unsqueeze(1).broadcast_to([128, hi - lo, L])
                tks[hf] = r.do(eng, lambda e, o=c3(KK, lo, hi),
                               i0=c3(VV_, lo, hi), i1=gbb:
                               e.tensor_mul(o, i0, i1),
                               waits=(t_gbe, t_bh, t_ka_last), inc=sem)
            t_tmp = tuple(tks)
            t_hh = None
            for ic in range(NIC):
                t_hh = r.do("vector", lambda e, o=x3v[:, ic, :],
                            d0=fm(LAM, ic, a, b), d1=fm(KK, ic, a, b),
                            i0=sti[ic]:
                            e.tensor_tensor_scan(o, d0, d1, i0,
                                                 AL.add, AL.subtract),
                            waits=(t_tmp[0] if ic < SPL else t_tmp[1],
                                   t_sti) + tuple(t_x3free), inc="dv")
            t_st2 = None
            if c == 0:
                t_st2 = r.do("vector", lambda e, i=x3v[:, :, L - 1]:
                             e.tensor_copy(STATE2[:, :], i),
                             waits=(t_hh,), inc="dv")
            tg2 = [None, None]
            th = [None, None]
            for hf, (eng, sem, lo, hi) in enumerate(
                    (("vector", "dv", 0, SPL), ("gpsimd", "po", SPL, NIC))):
                t = r.do(eng, lambda e, o=c3(GC, lo, hi),
                         i1=x3v[:, lo:hi, :]: e.tensor_mul(o, o, i1),
                         waits=(t_hh,), inc=sem)
                th[hf] = t
                t = r.do(eng, lambda e, o=c3(KK, lo, hi), i0=c3(ZS, lo, hi),
                         i1=c3(GC, lo, hi): e.tensor_mul(o, i0, i1),
                         waits=(t, t_zs[NIC - 1]), inc=sem)
                t = r.do(eng, lambda e,
                         o=lamb3[:, lo:hi, 2 * a: 2 * a + L],
                         i=c3(KK, lo, hi):
                         e.tensor_copy(o, i), waits=(t,), inc=sem)
                t = r.do(eng, lambda e,
                         o=lamb3[:, lo:hi, 2 * a + L: 2 * a + 2 * L],
                         i0=c3(KK, lo, hi),
                         i1=lamb3[:, lo:hi, 2 * a: 2 * a + L]:
                         e.tensor_tensor(o, i0, i1, AL.subtract),
                         waits=(t,), inc=sem)
                tg2[hf] = t
            return t_st2, tuple(tg2), tuple(th), t_tmp

        t_g0, ub0 = chain(UU, U2A, S0, t_u0, t_S, None)
        t_gbe0 = gamma_bcast(ub0, t_g0, 6, extra=(t_state,))
        sti0 = [STATE[:, ic:ic + 1] for ic in range(NIC)]
        t_st2, t_g20, t_h0, t_tmp0 = dhg(0, t_gbe0, sti0, t_state)
        t_g2c[0] = t_g20
        t_mvx = mm(pb[4][:, 0:1], XGT[:, :], ub0[:, :], True, True,
                   waits=(t_xgt, t_g0, drain.get(4)), inc="pe")
        fill[4] = t_mvx
        t_u1 = r.do("scalar", lambda e: e.activation(
            U2B[:, :], pb[4][:, 0:1], AF.Identity, scale=-1.0,
            bias=K1B[:, :]),
            waits=(t_mvx, t_k1), inc="ac")
        drain[4] = t_u1
        t_u1 = r.do("scalar", lambda e: e.activation(
            UU1[:, :], U2B[:, :], AF.Copy, scale=DVP1[:, :]),
            waits=(t_u1, t_dvp1), inc="ac")
        t_g1, ub1 = chain(UU1, U2B, S0B, t_u1, t_S1, None)
        t_gbe1 = gamma_bcast(ub1, t_g1, 6, extra=t_tmp0)
        sti1 = [STATE2[:, ic:ic + 1] for ic in range(NIC)]
        _, t_g21, t_h1, _ = dhg(1, t_gbe1, sti1, t_st2, t_x3free=t_h0)
        t_g2c[1] = t_g21
        t_h = t_h1
        t_gfull = t_g21
        if s1n is not None:
            s1n["zgate"] = list(t_g21)
        pump(4)
        t_state = r.do("vector", lambda e, i=fm3(GC, TT - 1, TT).squeeze(2):
                       e.tensor_copy(STATE[:, :], i),
                       waits=(t_state_pe,) + tuple(t_h), inc="dv")
        t_hprev = t_gfull
        t_gprev = t_state

        gtok = [None] * 4
        lamb3 = LAMb[:].rearrange("p (c t) -> p c t", c=NIC)
        wosp = [None, None, None]
        woout = [None, None, None]
        womm = [None, None, None]
        wofirst = [True, True, True]
        for dg in range(2):
            for ic in range(NIC):
                sl = (ic // 2) % 2
                if tile == 0:
                    p = (dg * NIC + ic) % 3
                    if p == 0:
                        slotf = WOU[:, 0:512]
                        wslotb = WOU.bitcast(BF16)[:, 2048:3072]
                    elif p == 1:
                        slotf = Y1[:, 0:512]
                        wslotb = Y1.bitcast(BF16)[:, 2048:3072]
                    else:
                        slotf = KSC[:, 0:512]
                        wslotb = KSC.bitcast(BF16)[:, 1024:2048]
                    jeng, jsem = "vector", "dv"
                    kw = (tjout2[1], tjsp[1]) if wofirst[1] and p == 1 else ()
                    if wofirst[2] and p == 2:
                        kw = tuple(ksl)
                    wofirst[p] = False
                    t_l = dma("sync", slotf, d_wo[dg, ic],
                              ("wo0", "wo1", "dff")[p],
                              waits=(wosp[p],) + kw)
                    t_js = r.do(jeng, lambda e, o=wslotb[:, 0:512], i=slotf:
                                e.tensor_copy(o, i),
                                waits=(t_l, womm[p], woout[p], wosp[p]) + kw,
                                inc=jsem)
                    wosp[p] = r.do(jeng, lambda e, o=wslotb[:, 512:1024],
                                   i=slotf, h=wslotb[:, 0:512]:
                                   e.tensor_tensor(o, i, h, AL.subtract),
                                   waits=(t_js,), inc=jsem)
                    dma("sync",
                        d_wos[dg, ic // 2][:, (ic % 2) * 1024:(ic % 2) * 1024 + 512],
                        wslotb[:, 0:512], ("dgo0", "dgo1", "dcon")[p],
                        waits=(wosp[p],))
                    woout[p] = dma("sync",
                                   d_wos[dg, ic // 2][:, (ic % 2) * 1024 + 512:
                                                      (ic % 2) * 1024 + 1024],
                                   wslotb[:, 512:1024],
                                   ("dgo0", "dgo1", "dcon")[p], waits=())
                    t_w = wosp[p]
                    split_done["wo"] = (woout[0], woout[1], woout[2],
                                        wosp[0], wosp[1], wosp[2])
                else:
                    slotb = WOU.bitcast(BF16)[:, sl * 2048:(sl + 1) * 2048]
                    if ic % 2 == 0:
                        t_wo2 = dma("sync", slotb[:, 0:1024],
                                    d_wos[dg, ic // 2][:, 0:1024],
                                    "wo%d" % sl,
                                    waits=(wou_free[sl],) + split_done["wo"])
                        t_wo2b = dma("sync", slotb[:, 1024:2048],
                                     d_wos[dg, ic // 2][:, 1024:2048],
                                     "wo%d" % (sl + 2), waits=())
                    t_w = t_wo2 if ic % 2 == 0 else t_wo2b
                    wslotb = slotb[:, (ic % 2) * 1024:(ic % 2 + 1) * 1024]
                for tc in range(TT // 128):
                    bank = 4 + 2 * dg + tc
                    gh = LAMb[:, ic * 2 * TT + tc * 256: ic * 2 * TT + tc * 256 + 128]
                    gl = LAMb[:, ic * 2 * TT + tc * 256 + 128:
                              ic * 2 * TT + tc * 256 + 256]
                    w0 = ((t_w, drain.get(bank)) + tuple(t_g2c[tc])) \
                        if ic == 0 else (t_w,)
                    mm(pb[bank][:, 0:512], gh, wslotb[:, 0:512],
                       ic == 0, False, waits=w0)
                    mm(pb[bank][:, 0:512], gl, wslotb[:, 0:512], False, False)
                    gtok[2 * dg + tc] = mm(pb[bank][:, 0:512], gh,
                                           wslotb[:, 512:1024],
                                           False, ic == NIC - 1, inc="pe")
                wou_free[sl] = gtok[2 * dg + (TT // 128) - 1]
                t_jmm = wou_free[sl]
                womm[(dg * NIC + ic) % 3] = t_jmm
                if ic % 4 == 3 and dg == 0:
                    pump(1)

        if tile == 0:
            split_done["stage"] = split_done["stage"] + (wosp[1], woout[1])
            ksl[0] = ksl[1] = wosp[2]
            ksl[2] = ksl[3] = woout[2]

        def ln(r1s, outs, wv, wa, slot):
            c = slot * 5
            su, sq, mn, vr, rs_ = (STT[:, c + i:c + i + 1] for i in range(5))
            t_sm = r.do("vector", lambda e, i=r1s, o=su: e.reduce_sum(o, i, X),
                        waits=wv, inc="dv")
            t_sq = r.do("scalar", lambda e, i=r1s, o=outs, q=sq: e.activation(
                o, i, AF.Square, accum_out=q), waits=wv + wa, inc="ac")
            t_mn = r.do("vector", lambda e, i=su, o=mn: e.tensor_scalar_mul(
                o, i, 1.0 / D), waits=(t_sm,), inc="dv")
            t_vr = r.do("vector", lambda e, i=mn, o=vr: e.tensor_mul(o, i, i),
                        waits=(t_mn,), inc="dv")
            t_vr = r.do("vector", lambda e, q=sq, o=vr: e.scalar_tensor_tensor(
                o, q, 1.0 / D, o, AL.mult, AL.subtract),
                waits=(t_vr, t_sq), inc="dv")
            t_vr = r.do("vector", lambda e, o=vr: e.tensor_scalar_add(o, o, 1e-5),
                        waits=(t_vr,), inc="dv")
            t_rs2 = r.do("scalar", lambda e, i=vr, o=rs_: e.activation(o, i, AF.Sqrt),
                         waits=(t_vr,), inc="ac")
            t_rs2 = r.do("vector", lambda e, o=rs_: e.reciprocal(o, o),
                         waits=(t_rs2,), inc="dv")
            return r.do("vector", lambda e, i=r1s, o=outs, m=mn, s=rs_:
                        e.tensor_scalar(o, i, m, s, AL.subtract, AL.mult),
                        waits=(t_rs2,), inc="dv")

        t_y1 = [None] * (TT // 128)
        txq_l = [None] * (TT // 128)
        t_xq = [None] * (TT // 128)
        txq = None
        for tc in range(TT // 128):
            r1s = R1[:, tc * D:(tc + 1) * D]
            y1s = Y1[:, tc * D:(tc + 1) * D]
            t_r1 = None
            for dg in range(2):
                bank = 4 + 2 * dg + tc
                fill[bank] = gtok[2 * dg + tc]
                t_r1 = r.do("vector", lambda e,
                            o=R1[:, tc * D + dg * 512: tc * D + (dg + 1) * 512],
                            p=pb[bank][:, 0:512],
                            x=XTM[:, xto + tc * D + dg * 512:
                                   xto + tc * D + (dg + 1) * 512]:
                            e.tensor_add(o, p, x),
                            waits=(gtok[2 * dg + tc], t_xtm, t_r1) + split_done["stage"],
                            inc="dv")
                drain[bank] = t_r1
            t_y1[tc] = ln(r1s, y1s, (t_r1,), split_done["stage"], 2 * tc)
            scr = XFM[:, xfo + tc * D:xfo + (tc + 1) * D]
            tq = r.do("vector", lambda e, o=r1s, i=y1s: e.tensor_scalar(
                o, i, RSC[:, :], 127.0, AL.mult, AL.min),
                waits=(t_y1[tc], t_rs, txq), inc="dv")
            tq = r.do("vector", lambda e, o=r1s: e.tensor_scalar(
                o, o, -127.0, 0.5, AL.max, AL.add), waits=(tq,), inc="dv")
            tq = r.do("vector", lambda e, o=FFSI[:, :], i=r1s: e.tensor_copy(o, i),
                      waits=(tq,) + split_done["stage"], inc="dv")
            tq = r.do("vector", lambda e, o=scr, i=FFSI[:, :]: e.tensor_copy(o, i),
                      waits=(tq,), inc="dv")
            tq = r.do("vector", lambda e, o=FFS[:, :], i0=scr, i1=r1s:
                      e.tensor_tensor(o, i0, i1, AL.is_gt),
                      waits=(tq,), inc="dv")
            txq = r.do("vector", lambda e, o=r1s, i0=scr, i1=FFS[:, :]:
                       e.tensor_sub(o, i0, i1), waits=(tq,), inc="dv")
            txq_l[tc] = txq
            pump(2 if tc == 0 else 1)
            for k8 in range(NDC):
                bank = 2 + k8 % 2
                t_tr = r.do("tensor", lambda e, i=r1s[:, k8 * 128:(k8 + 1) * 128],
                            o=pb[bank][:, 0:128]:
                            e.transpose(o, i, IDE[:, :]),
                            waits=(txq, drain.get(bank)), inc="pe")
                fill[bank] = t_tr
                t_ev = r.do("scalar", lambda e, o=XNT[:, k8 * 128:(k8 + 1) * 128],
                            p=pb[bank][:, 0:128]: e.activation(o, p, AF.Copy),
                            waits=(t_tr, t_xnt_free), inc="ac")
                drain[bank] = t_ev
            t_xq[tc] = t_ev
            ffb = [None, None]
            for oh in range(2):
                bank = 4 + 2 * oh + tc
                for k8 in range(NDC):
                    tok = mm(pb[bank][:, 0:512],
                             XNT[:, k8 * 128:(k8 + 1) * 128],
                             WFF[:, k8 * D + oh * 512: k8 * D + (oh + 1) * 512],
                             k8 == 0, k8 == NDC - 1,
                             waits=(t_xq[tc], t_wffn, drain.get(bank))
                             if k8 == 0 else (),
                             inc="pe" if k8 == NDC - 1 else None)
                fill[bank] = tok
                ffb[oh] = tok
                t_xnt_free = tok
            ffn_tok[tc] = ffb

        tos = []
        for tc in range(TT // 128):
            r1s = R1[:, tc * D:(tc + 1) * D]
            y1s = Y1[:, tc * D:(tc + 1) * D]
            x2s = XTM[:, xto + tc * D:xto + (tc + 1) * D]
            t_f = None
            for oh in range(2):
                bank = 4 + 2 * oh + tc
                t_f = r.do("vector", lambda e, o=r1s[:, oh * 512:(oh + 1) * 512],
                           p=pb[bank][:, 0:512], bb=BFN[:, oh * 512:(oh + 1) * 512]:
                           e.scalar_tensor_tensor(o, p, SB_[:, :], bb,
                                                  AL.mult, AL.add),
                           waits=(ffn_tok[tc][oh], t_s3, t_f), inc="dv")
                drain[bank] = t_f
            t_fa = r.do("vector", lambda e, o=r1s, i=y1s:
                        e.tensor_add(o, o, i), waits=(t_f,), inc="dv")
            t_y2[tc] = ln(r1s, x2s, (t_fa,), (), 2 * tc + 1)
            tos.append(dma("scalar", d_out[t0 + tc * 128: t0 + (tc + 1) * 128, :],
                           x2s, "do", waits=(t_y2[tc],)))
        xtm_free[0] = tos[-1]
        xfm_free[par] = txq_l[-1]
        nout += len(tos)
        if s1n is not None:
            while s1n["ec"] < 2 * NIC:
                emit_s1_group(s1n)
            s1 = s1n

    r.do("gpsimd", lambda e: e.nop(), waits=(("do", 16 * nout),))

    sems = {}
    for s in ("pe", "dv", "ac", "po", "dcon", "dff", "dgl0", "dgl1",
              "dgv0", "dgv1", "dgo0", "dgo1",
              "dxf0", "dxf1", "dxt",
              "do", "dw0", "dw1", "dw2", "dw3", "dw4", "dw5", "dw6", "dw7",
              "dl0", "dl1", "dl2", "dl3", "dl4", "dl5",
              "wo0", "wo1", "wo2", "wo3"):
        sems[s] = nc.alloc_semaphore(s)

    with nc.Block() as blk:
        def mk(eng):
            def body(e):
                for fn, waits, inc, by in r.ops[eng]:
                    for s, v in waits:
                        e.wait_ge(sems[s], v)
                    ins = fn(e)
                    if inc is not None:
                        ins.then_inc(sems[inc], by)
            return body
        blk.tensor(mk("tensor"))
        blk.vector(mk("vector"))
        blk.scalar(mk("scalar"))
        blk.gpsimd(mk("gpsimd"))
        blk.sync(mk("sync"))

    handles = dict(XI=XI, ZS=ZS, VV_=VV_, LAM=LAM, GC=GC, KK=KK, XFM=XFM,
                   XTM=XTM, R1=R1, Y1=Y1, XNT=XNT, WFF=WFF, STATE=STATE,
                   WOM2=WOM2, S0=S0, UU=UU, GB=GB, SB_=SB_, RSC=RSC)
    return nc, handles


def host_layout(inputs, T=2048):
    x = np.asarray(inputs["L0_attn_x"], np.float32)
    Win = np.asarray(inputs["L0_attn_Win"], np.float32)
    Wl = np.asarray(inputs["L0_attn_Wl"], np.float32)
    bl = np.asarray(inputs["L0_attn_bl"], np.float32)
    Wv = np.asarray(inputs["L0_attn_Wv"], np.float32)
    bv = np.asarray(inputs["L0_attn_bv"], np.float32)
    Wout = np.asarray(inputs["L0_attn_Wout"], np.float32)
    om = np.asarray(inputs["L0_attn_omega"], np.float32).reshape(-1)
    tern = np.asarray(inputs["L0_ffn_ternary_weight"], np.int32)
    bias = np.asarray(inputs["L0_ffn_bias"], np.float32)
    sc = np.asarray(inputs["L0_ffn_act_scale"], np.float32).reshape(1, 1)

    w_in = np.ascontiguousarray(
        Win.reshape(32, 128, 8, 128).transpose(0, 3, 2, 1).reshape(32, 128, 1024))
    w_l = np.ascontiguousarray(
        Wl.reshape(16, 128, 16, 128).transpose(0, 3, 2, 1).reshape(16, 128, 2048))
    w_v = np.ascontiguousarray(
        Wv.reshape(16, 128, 16, 128).transpose(0, 3, 2, 1).reshape(16, 128, 2048))
    w_out = np.ascontiguousarray(
        Wout.reshape(2, 512, 16, 128).transpose(0, 2, 3, 1))
    w_ffn = np.ascontiguousarray(tern.T)
    bl_fm = np.ascontiguousarray(bl.reshape(16, 128).T)
    bv_fm = np.ascontiguousarray(bv.reshape(16, 128).T)
    om_fm = np.ascontiguousarray(om.reshape(16, 128).T)
    b_ffn = np.ascontiguousarray(np.broadcast_to(bias, (128, 1024)))

    maps = []
    for b in range(x.shape[0]):
        maps.append({
            "x_fm": np.ascontiguousarray(x[b, :T].T),
            "x_tm": np.ascontiguousarray(x[b, :T]),
            "w_in": w_in, "w_l": w_l, "w_v": w_v, "w_out": w_out,
            "w_ffn": w_ffn, "bl_fm": bl_fm, "bv_fm": bv_fm, "om_fm": om_fm,
            "b_ffn": b_ffn, "scal": sc,
        })
    return maps


def kernel(**inputs):
    maps = host_layout(inputs, T=2048)
    nc, _ = build(n_tiles=8)
    res = run_bass_kernel_spmd(nc, maps, core_ids=list(range(8)))
    out = np.stack([np.asarray(res.results[b]["out"], np.float32)
                    for b in range(8)], axis=0)
    return out


# version 48
# speedup vs baseline: 1.0218x; 1.0218x over previous
import numpy as np
import concourse.bass as bass
import concourse.mybir as mybir
from concourse.bass_utils import run_bass_kernel_spmd

F32 = mybir.dt.float32
BF16 = mybir.dt.bfloat16
I32 = mybir.dt.int32
AL = mybir.AluOpType
AF = mybir.ActivationFunctionType
X = mybir.AxisListType.X

ENGS = ("tensor", "vector", "scalar", "gpsimd", "sync")
D = 1024
N = 2048
TT = 256
L = 128
NIC = N // 128
NDC = D // 128


class Rec:

    def __init__(self):
        self.ops = {e: [] for e in ENGS}
        self.semv = {}
        self.lastw = {e: {} for e in ENGS}

    def do(self, eng, fn, waits=(), inc=None, by=1):
        w = []
        for t in waits:
            if t is None:
                continue
            s, v = t
            if v > 0 and self.lastw[eng].get(s, 0) < v:
                w.append((s, v))
                self.lastw[eng][s] = v
        tok = None
        if inc is not None:
            self.semv[inc] = self.semv.get(inc, 0) + by
            tok = (inc, self.semv[inc])
        self.ops[eng].append((fn, tuple(w), inc, by))
        return tok


def build(n_tiles=8):
    T = n_tiles * TT
    nc = bass.Bass("TRN2", target_bir_lowering=False)
    r = Rec()

    d_xfm = nc.dram_tensor("x_fm", [D, T], F32, kind="ExternalInput")
    d_xtm = nc.dram_tensor("x_tm", [T, D], F32, kind="ExternalInput")
    d_win = nc.dram_tensor("w_in", [2 * NIC, 128, NDC * 128], F32, kind="ExternalInput")
    d_wl = nc.dram_tensor("w_l", [NIC, 128, N], F32, kind="ExternalInput")
    d_wv = nc.dram_tensor("w_v", [NIC, 128, N], F32, kind="ExternalInput")
    d_wo = nc.dram_tensor("w_out", [2, NIC, 128, 512], F32, kind="ExternalInput")
    d_wf = nc.dram_tensor("w_ffn", [D, D], I32, kind="ExternalInput")
    d_bl = nc.dram_tensor("bl_fm", [128, NIC], F32, kind="ExternalInput")
    d_bv = nc.dram_tensor("bv_fm", [128, NIC], F32, kind="ExternalInput")
    d_om = nc.dram_tensor("om_fm", [128, NIC], F32, kind="ExternalInput")
    d_bf = nc.dram_tensor("b_ffn", [128, D], F32, kind="ExternalInput")
    d_sc = nc.dram_tensor("scal", [1, 1], F32, kind="ExternalInput")
    d_out = nc.dram_tensor("out", [T, D], F32, kind="ExternalOutput")
    d_wls = nc.dram_tensor("wls", [NIC, 128, 2 * N], BF16, kind="Internal")
    d_wvs = nc.dram_tensor("wvs", [NIC, 128, 2 * N], BF16, kind="Internal")
    d_wos = nc.dram_tensor("wos", [2, NIC // 2, 128, 2048], BF16,
                           kind="Internal")
    c_id = nc.inline_tensor(np.eye(128, dtype=np.float32), "c_ident")
    c_sl = nc.inline_tensor(np.tril(np.ones((128, 128), np.float32), -1), "c_sl")
    c_sl2 = nc.inline_tensor(np.tril(np.full((128, 128), -2.0, np.float32), -1),
                             "c_sl2")
    c_on = nc.inline_tensor(np.ones((1, 128), np.float32), "c_ones")

    sb = nc.alloc_sbuf_tensor
    XI = sb("XI", [128, NIC * TT], F32)
    ZS = sb("ZS", [128, NIC * TT], F32)
    VV_ = sb("VB", [128, NIC * TT], F32)
    LAM = sb("LAM", [128, NIC * TT], F32)
    GC = sb("GC", [128, NIC * TT], F32)
    KK = sb("KK", [128, NIC * TT], F32)
    XFM = sb("XFM", [128, 2 * NDC * TT], F32)
    XTM = sb("XTM", [128, (TT // 128) * D], F32)
    R1 = sb("R1", [128, (TT // 128) * D], F32)
    Y1 = sb("Y1", [128, (TT // 128) * D], F32)
    XNT = sb("XNT", [128, NDC * 128], BF16)
    WFF = sb("WFF", [128, NDC * D], BF16)
    WIN = sb("WIN", [128, 2 * NDC * 128], F32)
    WLV = sb("WLV", [128, 2 * N], F32)
    WOU = sb("WOU", [128, 4 * 512], F32)
    ZRO = sb("ZRO", [128, TT], F32)
    S0 = sb("S0", [128, 128], F32)
    PA = sb("PA", [128, 128], F32)
    PB = sb("PB", [128, 128], F32)
    STA = sb("STA", [128, 128], F32)
    STB = sb("STB", [128, 128], F32)
    GB = sb("GB", [128, 128], F32)
    SCG = sb("SCG", [128, 128], F32)
    IDE = sb("IDE", [128, 128], F32)
    CSL = sb("CSL", [128, 128], F32)
    CSL2 = sb("CSL2", [128, 128], F32)
    CON = sb("CON", [1, 128], F32)
    STATE = sb("STATE", [128, NIC], F32)
    BLc = sb("BLc", [128, NIC], F32)
    BVc = sb("BVc", [128, NIC], F32)
    WOM = sb("WOM", [128, NIC], F32)
    WOM2 = sb("WOM2", [128, NIC], F32)
    WOMN = sb("WOMN", [128, NIC], F32)
    STATE2 = sb("STATE2", [128, NIC], F32)
    SB0 = sb("SB0", [128, NIC], F32)
    XG = sb("XG", [128, 128], F32)
    XGT = sb("XGT", [128, 128], F32)
    S0B = sb("S0B", [128, 128], F32)
    K1B = sb("K1B", [128, 1], F32)
    DVN1 = sb("DVN1", [128, 1], F32)
    DVP1 = sb("DVP1", [128, 1], F32)
    RHO1 = sb("RHO1", [128, 1], F32)
    UU1 = sb("UU1", [128, 1], F32)
    U2A = sb("U2A", [128, 1], F32)
    U2B = sb("U2B", [128, 1], F32)
    KSC = sb("KSC", [128, 4 * TT], F32)
    VVd = sb("VVd", [128, 1], F32)
    DVN = sb("DVN", [128, 1], F32)
    DVP = sb("DVP", [128, 1], F32)
    RHO = sb("RHO", [128, 1], F32)
    UU = sb("UU", [128, 1], F32)
    GROW = sb("GROW", [1, 128], F32)
    A1 = sb("A1", [1, 2], F32)
    SB_ = sb("SB_", [128, 1], F32)
    RSC = sb("RSC", [128, 1], F32)
    STT = sb("STT", [128, 20], F32)
    BFN = sb("BFN", [128, D], F32)
    FFS = sb("FFS", [128, D], F32)
    FFSI = FFS.bitcast(I32)

    XIb = XI.bitcast(BF16)
    WOUb_h = WOU.bitcast(BF16)
    LAMb = LAM.bitcast(BF16)

    pb = [nc.alloc_psum_tensor("pb%d" % i, [128, 512], F32) for i in range(8)]

    def fm(buf, ic, a, b):
        return buf[:, ic * TT + a: ic * TT + b]

    def fm3(buf, a, b):
        return buf[:].rearrange("p (c t) -> p c t", c=NIC)[:, :, a:b]

    drain = {}
    fill = {}

    def mm(bank_ap, lhsT, rhs, start, stop, waits=(), inc=None):
        def f(e, o=bank_ap, l=lhsT, rr=rhs, st=start, sp=stop):
            return e.matmul(o, l, rr, start=st, stop=sp)
        return r.do("tensor", f, waits=waits, inc=inc)

    def dma(eng, dst, src, sem, waits=()):
        def f(e, d=dst, s=src):
            return e.dma_start(d, s)
        return r.do(eng, f, waits=waits, inc=sem, by=16)

    dma("sync", IDE[:, :], c_id[:, :], "dcon")
    dma("sync", CSL[:, :], c_sl[:, :], "dcon")
    dma("sync", CSL2[:, :], c_sl2[:, :], "dcon")
    dma("sync", CON[:, :], c_on[:, :], "dcon")
    dma("sync", BLc[:, :], d_bl[:, :], "dcon")
    dma("sync", BVc[:, :], d_bv[:, :], "dcon")
    dma("sync", WOM[:, :], d_om[:, :], "dcon")
    dma("sync", BFN[:, :], d_bf[:, :], "dcon")
    t_pro = dma("sync", A1[0:1, 0:1], d_sc[:, :], "dcon")
    t_con = t_sc = t_pro
    stg = WIN.bitcast(I32)
    t_cv = None
    for k in range(NDC):
        sl = k % 2
        t_l = dma("sync", stg[:, sl * 1024:(sl + 1) * 1024],
                  d_wf[k * 128:(k + 1) * 128, :], "dff",
                  waits=(t_cv,))
        t_cv = r.do("vector", lambda e, a=stg[:, sl * 1024:(sl + 1) * 1024],
                    o=WFF[:, k * D:(k + 1) * D]:
                    e.tensor_scalar_add(o, a, -1.0),
                    waits=(t_l,), inc="dv")
    t_wffn = t_cv
    t1 = r.do("scalar", lambda e: e.activation(WOM2[:, :], WOM[:, :], AF.Exp),
              waits=(t_pro,), inc="ac")
    t1 = r.do("scalar", lambda e: e.activation(WOM2[:, :], WOM2[:, :], AF.Ln, bias=1.0),
              waits=(t1,), inc="ac")
    t_wom = r.do("vector", lambda e: e.tensor_scalar_mul(WOM2[:, :], WOM2[:, :], -8.0),
                 waits=(t1,), inc="dv")
    t_wom = r.do("vector", lambda e: e.tensor_scalar_mul(WOMN[:, :], WOM2[:, :], -1.0),
                 waits=(t_wom,), inc="dv")
    t_zro = r.do("vector", lambda e: e.memset(ZRO[:, :], 0.0), inc="dv")
    t_s1 = r.do("vector", lambda e: e.tensor_scalar_max(A1[0:1, 0:1], A1[0:1, 0:1], 1e-5),
                waits=(t_sc,), inc="dv")
    t_s2 = mm(pb[4][:, 0:1], CON[0:1, 0:128], A1[0:1, 0:1], True, True,
              waits=(t_s1, t_con), inc="pe")
    t_s3 = r.do("vector", lambda e: e.tensor_copy(SB_[:, :], pb[4][:, 0:1]),
                waits=(t_s2,), inc="dv")
    drain[4] = t_s3
    t_rs = r.do("vector", lambda e: e.reciprocal(RSC[:, :], SB_[:, :]),
                waits=(t_s3,), inc="dv")

    win_free = [None, None]
    wlv_free = [None, None]
    wou_free = [None] * 4
    xfm_free = [None, None]
    xtm_free = [None, None]
    nout = 0
    t_y2 = [None] * (TT // 128)
    t_state = None
    t_state_pe = None
    t_hprev = ()
    ksl = [None, None, None, None]
    tjsp = [None, None]
    tjout2 = [None, None]
    tjmm2 = [None, None]
    split_done = {"stage": (), "wl": (), "wv": (), "wo": ()}
    t_jsplit = None
    t_jmm = None
    t_jout = None
    ffn_tok = [None] * (TT // 128)
    t_xnt_free = None
    t_gprev = None

    def emit_front_loads(tile):
        t0 = tile * TT
        par = tile % 2
        xfo = par * NDC * TT
        t_xfm = None
        for dc in range(NDC):
            t_xfm = dma("gpsimd", XFM[:, xfo + dc * TT:xfo + (dc + 1) * TT],
                        d_xfm[dc * 128:(dc + 1) * 128, t0:t0 + TT],
                        "dxf%d" % par, waits=(xfm_free[par],))
        return dict(tile=tile, par=par, xfo=xfo, xto=0, t_xfm=t_xfm,
                    t_xi=[None] * NIC, t_zs=[None] * NIC, t_hi15=None,
                    t_s1done=None, ec=0, zgate=([] if tile == 0 else None))

    def emit_s1_group(st):
        ec = st["ec"]
        st["ec"] = ec + 1
        sl = ec % 2
        wslot = WIN[:, sl * 1024:(sl + 1) * 1024]
        t_w = dma("sync", wslot[:, 0:512], d_win[ec][:, 0:512], "dw%d" % sl,
                  waits=(win_free[sl], t_wffn))
        t_wb = dma("sync", wslot[:, 512:1024], d_win[ec][:, 512:1024],
                   "dw%d" % (sl + 2), waits=(win_free[sl], t_wffn))
        bank = ec % 4
        for dc in range(NDC):
            tok = mm(pb[bank][:, 0:TT], wslot[:, dc * 128:(dc + 1) * 128],
                     XFM[:, st["xfo"] + dc * TT:st["xfo"] + (dc + 1) * TT],
                     dc == 0, dc == NDC - 1,
                     waits=(t_w, st["t_xfm"], drain.get(bank)) if dc == 0
                     else (t_wb,) if dc == 4 else (),
                     inc="pe" if dc == NDC - 1 else None)
        fill[bank] = tok
        win_free[sl] = tok
        st["t_s1done"] = tok
        if ec < NIC:
            hi = XIb[:, ec * TT:(ec + 1) * TT]
            lo = XIb[:, NIC * TT + ec * TT: NIC * TT + (ec + 1) * TT]
            t_hi = r.do("scalar", lambda e, o=hi, i=pb[bank][:, 0:TT]:
                        e.activation(o, i, AF.Copy), waits=(tok,), inc="ac")
            st["t_hi15"] = t_hi
            tev = r.do("vector", lambda e, o=lo, i=pb[bank][:, 0:TT], h=hi:
                       e.tensor_tensor(o, i, h, AL.subtract),
                       waits=(t_hi,), inc="dv")
            st["t_xi"][ec] = tev
        else:
            dst = fm(ZS, ec - NIC, 0, TT)
            tev = r.do("scalar", lambda e, o=dst, i=pb[bank][:, 0:TT]:
                       e.activation(o, i, AF.Silu),
                       waits=(tok,) + tuple(st["zgate"] or ()), inc="ac")
            st["t_zs"][ec - NIC] = tev
        drain[bank] = tev

    s1 = emit_front_loads(0)
    while s1["ec"] < 2 * NIC:
        emit_s1_group(s1)
    for tile in range(n_tiles):
        t0 = tile * TT
        par = s1["par"]
        xfo = s1["xfo"]
        xto = s1["xto"]
        t_xi = s1["t_xi"]
        t_zs = s1["t_zs"]
        t_hi15 = s1["t_hi15"]
        t_xfm = s1["t_xfm"]
        t_s1done = s1["t_s1done"]
        for tc in range(TT // 128):
            t_xtm = dma("gpsimd", XTM[:, xto + tc * D:xto + (tc + 1) * D],
                        d_xtm[t0 + tc * 128:t0 + (tc + 1) * 128, :], "dxt",
                        waits=(xtm_free[0],))
        t_lam = [None] * NIC
        t_sig = [None] * NIC
        for jc in range(NIC):
            sl = jc % 2
            if tile == 0:
                p = jc % 2
                if p == 0:
                    slotf = WLV[:, 0:N]
                    wslotb = WLV.bitcast(BF16)[:, 2 * N:4 * N]
                else:
                    slotf = R1[:, 0:N]
                    wslotb = Y1.bitcast(BF16)[:, 0:2 * N]
                t_l = dma("sync", slotf, d_wl[jc], "dl%d" % p,
                          waits=(tjsp[p],))
                jeng, jsem = "vector", "dv"
                t_js = r.do(jeng, lambda e, o=wslotb[:, 0:N], i=slotf:
                            e.tensor_copy(o, i),
                            waits=(t_l, tjmm2[p], tjout2[p], tjsp[p]),
                            inc=jsem)
                tjsp[p] = r.do(jeng, lambda e, o=wslotb[:, N:2 * N], i=slotf,
                               h=wslotb[:, 0:N]:
                               e.tensor_tensor(o, i, h, AL.subtract),
                               waits=(t_js,), inc=jsem)
                dma("sync", d_wls[jc][:, 0:N], wslotb[:, 0:N], "dgl%d" % p,
                    waits=(tjsp[p],))
                tjout2[p] = dma("sync", d_wls[jc][:, N:2 * N],
                                wslotb[:, N:2 * N], "dgl%d" % p, waits=())
                t_w = tjsp[p]
                split_done["wl"] = (tjout2[0], tjout2[1])
            else:
                wslotb = WLV.bitcast(BF16)[:, sl * 2 * N:(sl + 1) * 2 * N]
                t_w = dma("sync", wslotb[:, 0:N // 2], d_wls[jc][:, 0:N // 2],
                          "dl%d" % sl,
                          waits=(wlv_free[sl],) + split_done["wl"])
                t_wq = dma("sync", wslotb[:, N // 2:N],
                           d_wls[jc][:, N // 2:N], "dl%d" % (sl + 4),
                           waits=(wlv_free[sl],) + split_done["wl"])
                t_w2 = dma("sync", wslotb[:, N:N + N // 2],
                           d_wls[jc][:, N:N + N // 2], "dl%d" % (sl + 2),
                           waits=(wlv_free[sl],) + split_done["wl"])
                t_w2b = dma("sync", wslotb[:, N + N // 2:2 * N],
                            d_wls[jc][:, N + N // 2:2 * N], "dl%d" % (sl + 6),
                            waits=(wlv_free[sl],) + split_done["wl"])
            if tile == 0:
                t_w2 = t_w
                t_w2b = t_w
                t_wq = t_w
            bank = (jc // 2) % 4
            pcol = (jc % 2) * TT
            for ic in range(NIC):
                wh = wslotb[:, ic * 128:(ic + 1) * 128]
                xih = XIb[:, ic * TT:(ic + 1) * TT]
                w0 = (t_w, t_hi15, drain.get(bank)) if ic == 0 and jc % 2 == 0 \
                    else (t_w,) if ic == 0 else (t_wq,) if ic == 8 else ()
                mm(pb[bank][:, pcol:pcol + TT], wh, xih,
                   ic == 0 and jc % 2 == 0, False, waits=w0)
            for ic in range(NIC):
                wh = wslotb[:, ic * 128:(ic + 1) * 128]
                xil = XIb[:, NIC * TT + ic * TT: NIC * TT + (ic + 1) * TT]
                mm(pb[bank][:, pcol:pcol + TT], wh, xil, False, False,
                   waits=(t_xi[NIC - 1],) if ic == 0 else ())
            for ic in range(NIC):
                wl_ = wslotb[:, N + ic * 128: N + (ic + 1) * 128]
                xih = XIb[:, ic * TT:(ic + 1) * TT]
                tok = mm(pb[bank][:, pcol:pcol + TT], wl_, xih,
                         False, ic == NIC - 1 and jc % 2 == 1,
                         waits=(t_w2,) if ic == 0
                         else (t_w2b,) if ic == 8 else (),
                         inc="pe" if ic == NIC - 1 else None)
            fill[bank] = tok
            wlv_free[sl] = tok
            t_jmm = tok
            tjmm2[jc % 2] = tok
            if jc % 2 == 1:
                for j2 in (jc - 1, jc):
                    tev = r.do("scalar", lambda e, o=fm(KK, j2, 0, TT),
                               i=pb[bank][:, (j2 % 2) * TT:(j2 % 2) * TT + TT],
                               b=BLc[:, j2:j2 + 1]:
                               e.activation(o, i, AF.Sigmoid, bias=b),
                               waits=(tok,) + tuple(t_hprev), inc="ac")
                    t_sig[j2] = tev
                drain[bank] = tev
        t_lami = [None] * NIC
        for jc in range(NIC):
            t_lam[jc] = r.do("scalar", lambda e, o=fm(LAM, jc, 0, TT),
                             i=fm(KK, jc, 0, TT), s=WOM2[:, jc:jc + 1]:
                             e.activation(o, i, AF.Exp, scale=s),
                             waits=(t_wom, t_sig[jc], t_gprev), inc="ac")
        for jc in range(NIC):
            t_lami[jc] = r.do("scalar", lambda e, o=fm(KK, jc, 0, TT),
                              s=WOMN[:, jc:jc + 1]:
                              e.activation(o, o, AF.Exp, scale=s),
                              waits=(t_lam[jc],), inc="ac")

        t_gc = [None] * NIC
        t_rr = [None] * NIC
        tE = None

        def emit_gauge_ic(ic):
            nonlocal tE
            t_ms = None
            if tile == 0:
                t_ms = r.do("vector", lambda e, o=fm(LAM, ic, 0, 1):
                            e.memset(o, 1.0), waits=(t_lam[ic], tE), inc="dv")
                t_ms = r.do("vector", lambda e, o=fm(KK, ic, 0, 1):
                            e.memset(o, 1.0), waits=(t_lami[ic], t_ms),
                            inc="dv")
            tE = t_gc[ic] = r.do(
                "vector", lambda e, o=fm(GC, ic, 0, TT), d0=fm(LAM, ic, 0, TT):
                e.tensor_tensor_scan(o, d0, ZRO[:, 0:TT], 1.0, AL.mult, AL.add),
                waits=(t_lam[ic], t_zro, t_ms, tE, t_gprev), inc="dv")
            tE = t_rr[ic] = r.do(
                "vector", lambda e, o=fm(KK, ic, 0, TT):
                e.tensor_tensor_scan(o, o, ZRO[:, 0:TT], 1.0, AL.mult, AL.add),
                waits=(t_lami[ic], tE), inc="dv")
            tE = r.do("vector", lambda e, o=fm(LAM, ic, 1, TT),
                      i0=fm(KK, ic, 1, TT), i1=fm(KK, ic, 0, TT - 1):
                      e.tensor_sub(o, i0, i1), waits=(tE,), inc="dv")
            tE = r.do("vector", lambda e, o=fm(LAM, ic, 0, 1),
                      i=fm(KK, ic, 0, 1): e.tensor_scalar_add(o, i, -1.0),
                      waits=(tE,), inc="dv")
            rec = XFM[:, xfo + (ic % NDC) * TT:xfo + (ic % NDC) * TT + TT]
            tE = r.do("vector", lambda e, o=rec,
                      i0=XIb[:, ic * TT:(ic + 1) * TT],
                      i1=XIb[:, NIC * TT + ic * TT:NIC * TT + (ic + 1) * TT]:
                      e.tensor_add(o, i0, i1),
                      waits=(tE, t_s1done, t_xi[ic]), inc="dv")
            tE = r.do("vector", lambda e, o=fm(LAM, ic, 0, TT), i1=rec:
                      e.tensor_mul(o, o, i1), waits=(tE,), inc="dv")

        if tile != 0:
            for ic in range(NIC):
                emit_gauge_ic(ic)
        t_bh = tE

        t_rb = None
        if tile != 0:
            t_rb = r.do("vector", lambda e: e.reduce_sum(
                SB0[:, :].unsqueeze(2), fm3(LAM, 0, L), X),
                waits=(t_bh, t_state_pe), inc="dv")
            t_rb = r.do("vector", lambda e: e.tensor_add(
                SB0[:, :], SB0[:, :], STATE[:, :]),
                waits=(t_rb, t_state), inc="dv")

        t_v = [None] * NIC
        t_ka_last = None
        kat = {}

        def emit_ka_ic(ic):
            nonlocal t_ka_last
            sl_ = ic % 4
            ks = KSC[:, sl_ * TT:(sl_ + 1) * TT]
            tk = r.do("vector", lambda e, o=ks[:, 0:1], i=fm(VV_, ic, 0, 1):
                      e.tensor_copy(o, i),
                      waits=(t_v[ic], t_gc[ic], ksl[sl_]), inc="dv")
            tk = r.do("vector", lambda e, o=ks[:, 1:TT],
                      i0=fm(GC, ic, 0, TT - 1), i1=fm(VV_, ic, 1, TT):
                      e.tensor_mul(o, i0, i1), waits=(tk,), inc="dv")
            ta = r.do("vector", lambda e, o=fm(VV_, ic, 1, TT),
                      i1=fm(KK, ic, 0, TT - 1): e.tensor_mul(o, o, i1),
                      waits=(tk, t_rr[ic]), inc="dv")
            kat[ic] = ta
            t_ka_last = ta

        def emit_gram_ic(ic):
            sl_ = ic % 4
            ks = KSC[:, sl_ * TT:(sl_ + 1) * TT]
            ta = kat[ic]
            w0 = (ta, t_state, t_rb, drain.get(4), drain.get(5),
                  drain.get(6), drain.get(7)) if ic == 0 else (ta,)
            mm(pb[4][:, 0:128], ks[:, 0:L], fm(VV_, ic, 0, L),
               ic == 0, ic == NIC - 1, waits=w0)
            mm(pb[5][:, 0:1], ks[:, 0:L], STATE[:, ic:ic + 1],
               ic == 0, False)
            mm(pb[5][:, 1:129], ks[:, 0:L], fm(LAM, ic, 0, L),
               False, False)
            mm(pb[5][:, 129:130], ks[:, L:TT], SB0[:, ic:ic + 1],
               False, ic == NIC - 1)
            mm(pb[6][:, 0:128], ks[:, L:TT], fm(VV_, ic, L, TT),
               ic == 0, ic == NIC - 1)
            mm(pb[7][:, 0:128], ks[:, L:TT], fm(LAM, ic, L, TT),
               ic == 0, False)
            tok = mm(pb[7][:, 128:256], ks[:, L:TT],
                     fm(VV_, ic, 0, L), False, ic == NIC - 1, inc="pe")
            ksl[sl_] = tok
            return tok

        for jc in range(NIC):
            sl = jc % 2
            if tile == 0:
                p = jc % 2
                if p == 0:
                    slotf = WLV[:, 0:N]
                    wslotb = WLV.bitcast(BF16)[:, 2 * N:4 * N]
                else:
                    slotf = R1[:, 0:N]
                    wslotb = Y1.bitcast(BF16)[:, 0:2 * N]
                t_l = dma("sync", slotf, d_wv[jc], "dl%d" % p,
                          waits=(tjsp[p],))
                jeng, jsem = "vector", "dv"
                t_js = r.do(jeng, lambda e, o=wslotb[:, 0:N], i=slotf:
                            e.tensor_copy(o, i),
                            waits=(t_l, tjmm2[p], tjout2[p], tjsp[p]),
                            inc=jsem)
                tjsp[p] = r.do(jeng, lambda e, o=wslotb[:, N:2 * N], i=slotf,
                               h=wslotb[:, 0:N]:
                               e.tensor_tensor(o, i, h, AL.subtract),
                               waits=(t_js,), inc=jsem)
                dma("sync", d_wvs[jc][:, 0:N], wslotb[:, 0:N], "dgv%d" % p,
                    waits=(tjsp[p],))
                tjout2[p] = dma("sync", d_wvs[jc][:, N:2 * N],
                                wslotb[:, N:2 * N], "dgv%d" % p, waits=())
                t_w = tjsp[p]
                split_done["wv"] = (tjout2[0], tjout2[1])
            else:
                wslotb = WLV.bitcast(BF16)[:, sl * 2 * N:(sl + 1) * 2 * N]
                t_w = dma("sync", wslotb[:, 0:N // 2], d_wvs[jc][:, 0:N // 2],
                          "dl%d" % sl,
                          waits=(wlv_free[sl],) + split_done["wv"])
                t_wq = dma("sync", wslotb[:, N // 2:N],
                           d_wvs[jc][:, N // 2:N], "dl%d" % (sl + 4),
                           waits=(wlv_free[sl],) + split_done["wv"])
                t_w2 = dma("sync", wslotb[:, N:N + N // 2],
                           d_wvs[jc][:, N:N + N // 2], "dl%d" % (sl + 2),
                           waits=(wlv_free[sl],) + split_done["wv"])
                t_w2b = dma("sync", wslotb[:, N + N // 2:2 * N],
                            d_wvs[jc][:, N + N // 2:2 * N], "dl%d" % (sl + 6),
                            waits=(wlv_free[sl],) + split_done["wv"])
            if tile == 0:
                t_w2 = t_w
                t_w2b = t_w
                t_wq = t_w
            bank = (jc // 2) % 4
            pcol = (jc % 2) * TT
            for ic in range(NIC):
                wh = wslotb[:, ic * 128:(ic + 1) * 128]
                xih = XIb[:, ic * TT:(ic + 1) * TT]
                w0 = (t_w, drain.get(bank)) if ic == 0 and jc % 2 == 0 \
                    else (t_w,) if ic == 0 else (t_wq,) if ic == 8 else ()
                mm(pb[bank][:, pcol:pcol + TT], wh, xih,
                   ic == 0 and jc % 2 == 0, False, waits=w0)
            for ic in range(NIC):
                wh = wslotb[:, ic * 128:(ic + 1) * 128]
                xil = XIb[:, NIC * TT + ic * TT: NIC * TT + (ic + 1) * TT]
                mm(pb[bank][:, pcol:pcol + TT], wh, xil, False, False,
                   waits=(t_xi[NIC - 1],) if ic == 0 else ())
            for ic in range(NIC):
                wl_ = wslotb[:, N + ic * 128: N + (ic + 1) * 128]
                xih = XIb[:, ic * TT:(ic + 1) * TT]
                tok = mm(pb[bank][:, pcol:pcol + TT], wl_, xih,
                         False, ic == NIC - 1 and jc % 2 == 1,
                         waits=(t_w2,) if ic == 0
                         else (t_w2b,) if ic == 8 else (),
                         inc="pe" if ic == NIC - 1 else None)
            fill[bank] = tok
            wlv_free[sl] = tok
            t_jmm = tok
            tjmm2[jc % 2] = tok
            if tile == 0:
                emit_gauge_ic(jc)
            if jc % 2 == 1:
                for j2 in (jc - 1, jc):
                    tev = r.do("scalar", lambda e, o=fm(VV_, j2, 0, TT),
                               i=pb[bank][:, (j2 % 2) * TT:(j2 % 2) * TT + TT],
                               b=BVc[:, j2:j2 + 1]:
                               e.activation(o, i, AF.Identity, bias=b),
                               waits=(tok,), inc="ac")
                    t_v[j2] = tev
                drain[bank] = tev
        if tile == 0:
            t_bh = tE
        if tile == 0:
            split_done["stage"] = split_done["stage"] + (tjsp[1], tjout2[1])
            split_done["wl"] = split_done["wl"] + split_done["wv"] + \
                (tjsp[0], tjsp[1])
        if tile != 0:
            for ic in range(NIC):
                emit_ka_ic(ic)
                t_gram = emit_gram_ic(ic)

        t_b = None
        if tile == 0:
            xibh = XIb[:, 0:NIC * TT].rearrange("p (c t) -> p c t", c=NIC)
            xibl = XIb[:, NIC * TT:2 * NIC * TT].rearrange("p (c t) -> p c t",
                                                           c=NIC)
            t_rc = r.do("vector", lambda e, i0=xibh[:, :, 0:1].squeeze(2),
                        i1=xibl[:, :, 0:1].squeeze(2):
                        e.tensor_add(STATE2[:, :], i0, i1),
                        waits=(t_xi[NIC - 1],), inc="dv")
            for ic in range(NIC):
                w0 = (t_v[NIC - 1], t_rc, drain.get(4), t_con) if ic == 0 else ()
                mm(pb[4][0:1, 0:1], fm(VV_, ic, 0, 1), STATE2[:, ic:ic + 1],
                   ic == 0, False, waits=w0)
                tok = mm(pb[4][0:1, 1:2], fm(VV_, ic, 0, 1), fm(VV_, ic, 0, 1),
                         False, ic == NIC - 1,
                         inc="pe" if ic == NIC - 1 else None)
            fill[4] = tok
            t_a = r.do("vector", lambda e: e.reciprocal(A1[0:1, 0:1], pb[4][0:1, 1:2]),
                       waits=(tok,), inc="dv")
            t_a = r.do("vector", lambda e: e.tensor_mul(A1[0:1, 0:1], A1[0:1, 0:1],
                                                        pb[4][0:1, 0:1]),
                       waits=(t_a,), inc="dv")
            t_a = r.do("vector", lambda e: e.tensor_scalar_mul(A1[0:1, 0:1],
                                                               A1[0:1, 0:1], -2.0),
                       waits=(t_a,), inc="dv")
            drain[4] = t_a
            t_b = mm(pb[5][:, 0:1], CON[0:1, 0:128], A1[0:1, 0:1], True, True,
                     waits=(t_a, drain.get(5)), inc="pe")
            fill[5] = t_b
            t_ab = r.do("vector", lambda e: e.tensor_copy(GB[:, 0:1], pb[5][:, 0:1]),
                        waits=(t_b,), inc="dv")
            drain[5] = t_ab
            t_state = r.do("vector", lambda e: e.scalar_tensor_tensor(
                STATE[:, :], fm3(VV_, 0, 1).squeeze(2), GB[:, 0:1],
                STATE2[:, :], AL.mult, AL.add),
                waits=(t_ab, t_bh, t_rc), inc="dv")

        if tile == 0:
            t_rb = r.do("vector", lambda e: e.reduce_sum(
                SB0[:, :].unsqueeze(2), fm3(LAM, 0, L), X),
                waits=(t_bh,), inc="dv")
            t_rb = r.do("vector", lambda e: e.tensor_add(
                SB0[:, :], SB0[:, :], STATE[:, :]),
                waits=(t_rb, t_state), inc="dv")
            for ic in range(NIC):
                emit_ka_ic(ic)
                t_gram = emit_gram_ic(ic)

        t_g2c = [None] * (TT // L)
        s1n = emit_front_loads(tile + 1) if tile + 1 < n_tiles else None

        def pump(n=1):
            for _ in range(n):
                if s1n is None or s1n["ec"] >= 2 * NIC:
                    return
                if s1n["ec"] >= NIC and s1n["zgate"] is None:
                    return
                emit_s1_group(s1n)

        t_state_pe = t_gram
        fill[4] = fill[5] = fill[6] = fill[7] = t_gram
        t_vv = r.do("vector", lambda e: e.tensor_mul(
            SCG[:, :], pb[4][:, 0:128], IDE[:, :]), waits=(t_gram,), inc="dv")
        t_vv = r.do("vector", lambda e: e.reduce_sum(VVd[:, :], SCG[:, :], X),
                    waits=(t_vv,), inc="dv")
        t_dvn = r.do("vector", lambda e: e.reciprocal(DVN[:, :], VVd[:, :]),
                     waits=(t_vv,), inc="dv")
        t_S = r.do("vector", lambda e: e.scalar_tensor_tensor(
            S0[:, :], pb[4][:, 0:128], DVN[:, :], CSL2[:, :], AL.mult, AL.mult),
            waits=(t_dvn,), inc="dv")
        drain[4] = t_S
        t_vv1 = r.do("vector", lambda e: e.tensor_mul(
            SCG[:, :], pb[6][:, 0:128], IDE[:, :]), waits=(t_gram, t_vv),
            inc="dv")
        t_vv1 = r.do("vector", lambda e: e.reduce_sum(VVd[:, :], SCG[:, :], X),
                     waits=(t_vv1,), inc="dv")
        t_dvn1 = r.do("vector", lambda e: e.reciprocal(DVN1[:, :], VVd[:, :]),
                      waits=(t_vv1,), inc="dv")
        t_S1 = r.do("vector", lambda e: e.scalar_tensor_tensor(
            S0B[:, :], pb[6][:, 0:128], DVN1[:, :], CSL2[:, :],
            AL.mult, AL.mult), waits=(t_dvn1,), inc="dv")
        drain[6] = t_S1
        t_dvp = r.do("vector", lambda e: e.tensor_scalar_mul(DVP[:, :],
                                                             DVN[:, :], 2.0),
                     waits=(t_dvn,), inc="dv")
        t_dvp1 = r.do("vector", lambda e: e.tensor_scalar_mul(DVP1[:, :],
                                                              DVN1[:, :], 2.0),
                      waits=(t_dvn1,), inc="dv")
        t_rho = r.do("vector", lambda e: e.tensor_mul(
            SCG[:, :], pb[5][:, 1:129], CSL[:, :]), waits=(t_gram, t_vv1),
            inc="dv")
        t_rho = r.do("vector", lambda e: e.reduce_sum(RHO[:, :], SCG[:, :], X),
                     waits=(t_rho,), inc="dv")
        t_u0 = r.do("vector", lambda e: e.scalar_tensor_tensor(
            UU[:, :], pb[5][:, 0:1], RHO[:, :], DVP[:, :], AL.add, AL.mult),
            waits=(t_rho, t_dvp), inc="dv")
        drain[5] = t_u0
        t_rho1 = r.do("vector", lambda e: e.tensor_mul(
            SCG[:, :], pb[7][:, 0:128],
            CSL[:, :]), waits=(t_gram, t_rho), inc="dv")
        t_rho1 = r.do("vector", lambda e: e.reduce_sum(RHO1[:, :], SCG[:, :], X),
                      waits=(t_rho1,), inc="dv")
        drain[7] = t_rho1
        t_k1 = r.do("scalar", lambda e: e.activation(
            K1B[:, :], pb[5][:, 129:130], AF.Identity, bias=RHO1[:, :]),
            waits=(t_gram, t_rho1), inc="ac")
        t_xg = r.do("scalar", lambda e: e.activation(
            XG[:, :], pb[7][:, 128:256], AF.Copy),
            waits=(t_gram, t_k1), inc="ac")
        drain[7] = t_xg
        drain[4] = t_S
        t_xgt = r.do("tensor", lambda e: e.transpose(pb[5][:, 0:128], XG[:, :],
                                                     IDE[:, :]),
                     waits=(t_xg, t_u0, t_k1), inc="pe")
        fill[5] = t_xgt
        t_xgt = r.do("scalar", lambda e: e.activation(
            XGT[:, :], pb[5][:, 0:128], AF.Copy), waits=(t_xgt,), inc="ac")
        drain[5] = t_xgt
        drain[6] = t_S1
        pump(7)

        def chain(uu, u2, sbuf_S, t_u, t_Ssrc, dvp_unused):
            cs = None
            pcur = sbuf_S
            prev_p = t_Ssrc
            prev_s = None
            t_tp = r.do("tensor", lambda e, i=sbuf_S: e.transpose(
                pb[7][:, 0:128], i[:, :], IDE[:, :]),
                waits=(t_Ssrc, drain.get(7)), inc="pe")
            fill[7] = t_tp
            t_st_ = r.do("scalar", lambda e: e.activation(
                STA[:, :], pb[7][:, 0:128], AF.Copy),
                waits=(t_tp,), inc="ac")
            drain[7] = t_st_
            cs = STA
            prev_s = t_st_
            t_g = t_u
            cur, nxt = uu, u2
            for k in range(7):
                t_mv = mm(pb[5][:, 0:1], cs[:, :], cur[:, :], True, True,
                          waits=(prev_s, t_g, drain.get(5)), inc="pe")
                fill[5] = t_mv
                t_g = r.do("scalar", lambda e, p=pb[5][:, 0:1], o=nxt, b=cur:
                           e.activation(o[:, :], p, AF.Identity, bias=b[:, :]),
                           waits=(t_mv, t_g), inc="ac")
                drain[5] = t_g
                cur, nxt = nxt, cur
                pump(1)
                if k == 6:
                    break
                nxt_p = PB if pcur in (sbuf_S, PA) else PA
                nxt_s = STB if cs is STA else STA
                t_sq = mm(pb[4][:, 0:128], cs[:, :], pcur[:, :], True, True,
                          waits=(prev_p, prev_s, drain.get(4)), inc="pe")
                fill[4] = t_sq
                t_ep = r.do("scalar", lambda e, o=nxt_p: e.activation(
                    o[:, :], pb[4][:, 0:128], AF.Copy),
                    waits=(t_sq,), inc="ac")
                drain[4] = t_ep
                t_tp = r.do("tensor", lambda e, i=nxt_p: e.transpose(
                    pb[7][:, 0:128], i[:, :], IDE[:, :]),
                    waits=(t_ep, drain.get(7)), inc="pe")
                fill[7] = t_tp
                t_es = r.do("scalar", lambda e, o=nxt_s: e.activation(
                    o[:, :], pb[7][:, 0:128], AF.Copy),
                    waits=(t_tp,), inc="ac")
                drain[7] = t_es
                pcur, cs = nxt_p, nxt_s
                prev_p, prev_s = t_ep, t_es
            return t_g, cur

        def gamma_bcast(uu, t_g, bank, extra=()):
            t_tg = r.do("tensor", lambda e, u=uu: e.transpose(
                pb[7][0:1, 0:128], u[:, :], IDE[:, :]),
                waits=(t_g, drain.get(7)), inc="pe")
            fill[7] = t_tg
            t_gr = r.do("scalar", lambda e: e.activation(
                GROW[:, :], pb[7][0:1, 0:128], AF.Copy),
                waits=(t_tg,), inc="ac")
            drain[7] = t_gr
            t_gb = mm(pb[bank][:, 0:128], CON[0:1, 0:128], GROW[0:1, 0:128],
                      True, True, waits=(t_gr, drain.get(bank)), inc="pe")
            fill[bank] = t_gb
            t_gbe = r.do("scalar", lambda e: e.activation(
                GB[:, :], pb[bank][:, 0:128], AF.Copy),
                waits=(t_gb,) + tuple(extra), inc="ac")
            drain[bank] = t_gbe
            return t_gbe

        x3v = XFM[:, xfo:xfo + NIC * L].rearrange("p (c t) -> p c t", c=NIC)

        SPL = 12

        def dhg(c, t_gbe, sti, t_sti, t_x3free=()):
            a, b = c * L, (c + 1) * L

            def c3(buf, lo, hi):
                return buf[:].rearrange("p (c t) -> p c t",
                                        c=NIC)[:, lo:hi, a:b]

            lamb3 = LAMb[:].rearrange("p (c t) -> p c t", c=NIC)
            tks = [None, None]
            for hf, (eng, sem, lo, hi) in enumerate(
                    (("vector", "dv", 0, SPL), ("gpsimd", "po", SPL, NIC))):
                gbb = GB[:].unsqueeze(1).broadcast_to([128, hi - lo, L])
                tks[hf] = r.do(eng, lambda e, o=c3(KK, lo, hi),
                               i0=c3(VV_, lo, hi), i1=gbb:
                               e.tensor_mul(o, i0, i1),
                               waits=(t_gbe, t_bh, t_ka_last), inc=sem)
            t_tmp = tuple(tks)
            t_hh = None
            for ic in range(NIC):
                t_hh = r.do("vector", lambda e, o=x3v[:, ic, :],
                            d0=fm(LAM, ic, a, b), d1=fm(KK, ic, a, b),
                            i0=sti[ic]:
                            e.tensor_tensor_scan(o, d0, d1, i0,
                                                 AL.add, AL.subtract),
                            waits=(t_tmp[0] if ic < SPL else t_tmp[1],
                                   t_sti) + tuple(t_x3free), inc="dv")
            t_st2 = None
            if c == 0:
                t_st2 = r.do("vector", lambda e, i=x3v[:, :, L - 1]:
                             e.tensor_copy(STATE2[:, :], i),
                             waits=(t_hh,), inc="dv")
            tg2 = [None, None]
            th = [None, None]
            for hf, (eng, sem, lo, hi) in enumerate(
                    (("vector", "dv", 0, SPL), ("gpsimd", "po", SPL, NIC))):
                t = r.do(eng, lambda e, o=c3(GC, lo, hi),
                         i1=x3v[:, lo:hi, :]: e.tensor_mul(o, o, i1),
                         waits=(t_hh,), inc=sem)
                th[hf] = t
                t = r.do(eng, lambda e, o=c3(KK, lo, hi), i0=c3(ZS, lo, hi),
                         i1=c3(GC, lo, hi): e.tensor_mul(o, i0, i1),
                         waits=(t, t_zs[NIC - 1]), inc=sem)
                t = r.do(eng, lambda e,
                         o=lamb3[:, lo:hi, 2 * a: 2 * a + L],
                         i=c3(KK, lo, hi):
                         e.tensor_copy(o, i), waits=(t,), inc=sem)
                t = r.do(eng, lambda e,
                         o=lamb3[:, lo:hi, 2 * a + L: 2 * a + 2 * L],
                         i0=c3(KK, lo, hi),
                         i1=lamb3[:, lo:hi, 2 * a: 2 * a + L]:
                         e.tensor_tensor(o, i0, i1, AL.subtract),
                         waits=(t,), inc=sem)
                tg2[hf] = t
            return t_st2, tuple(tg2), tuple(th), t_tmp

        t_g0, ub0 = chain(UU, U2A, S0, t_u0, t_S, None)
        t_gbe0 = gamma_bcast(ub0, t_g0, 6, extra=(t_state,))
        sti0 = [STATE[:, ic:ic + 1] for ic in range(NIC)]
        t_st2, t_g20, t_h0, t_tmp0 = dhg(0, t_gbe0, sti0, t_state)
        t_g2c[0] = t_g20
        t_mvx = mm(pb[4][:, 0:1], XGT[:, :], ub0[:, :], True, True,
                   waits=(t_xgt, t_g0, drain.get(4)), inc="pe")
        fill[4] = t_mvx
        t_u1 = r.do("scalar", lambda e: e.activation(
            U2B[:, :], pb[4][:, 0:1], AF.Identity, scale=-1.0,
            bias=K1B[:, :]),
            waits=(t_mvx, t_k1), inc="ac")
        drain[4] = t_u1
        t_u1 = r.do("scalar", lambda e: e.activation(
            UU1[:, :], U2B[:, :], AF.Copy, scale=DVP1[:, :]),
            waits=(t_u1, t_dvp1), inc="ac")
        t_g1, ub1 = chain(UU1, U2B, S0B, t_u1, t_S1, None)
        t_gbe1 = gamma_bcast(ub1, t_g1, 6, extra=t_tmp0)
        sti1 = [STATE2[:, ic:ic + 1] for ic in range(NIC)]
        _, t_g21, t_h1, _ = dhg(1, t_gbe1, sti1, t_st2, t_x3free=t_h0)
        t_g2c[1] = t_g21
        t_h = t_h1
        t_gfull = t_g21
        if s1n is not None:
            s1n["zgate"] = list(t_g21)
        pump(4)
        t_state = r.do("vector", lambda e, i=fm3(GC, TT - 1, TT).squeeze(2):
                       e.tensor_copy(STATE[:, :], i),
                       waits=(t_state_pe,) + tuple(t_h), inc="dv")
        t_hprev = t_gfull
        t_gprev = t_state

        gtok = [None] * 4
        lamb3 = LAMb[:].rearrange("p (c t) -> p c t", c=NIC)
        wosp = [None, None, None]
        woout = [None, None, None]
        womm = [None, None, None]
        wofirst = [True, True, True]
        for dg in range(2):
            for ic in range(NIC):
                sl = (ic // 2) % 2
                if tile == 0:
                    p = (dg * NIC + ic) % 3
                    if p == 0:
                        slotf = WOU[:, 0:512]
                        wslotb = WOU.bitcast(BF16)[:, 2048:3072]
                    elif p == 1:
                        slotf = Y1[:, 0:512]
                        wslotb = Y1.bitcast(BF16)[:, 2048:3072]
                    else:
                        slotf = KSC[:, 0:512]
                        wslotb = KSC.bitcast(BF16)[:, 1024:2048]
                    jeng, jsem = "vector", "dv"
                    kw = (tjout2[1], tjsp[1]) if wofirst[1] and p == 1 else ()
                    if wofirst[2] and p == 2:
                        kw = tuple(ksl)
                    wofirst[p] = False
                    t_l = dma("sync", slotf, d_wo[dg, ic],
                              ("wo0", "wo1", "dff")[p],
                              waits=(wosp[p],) + kw)
                    t_js = r.do(jeng, lambda e, o=wslotb[:, 0:512], i=slotf:
                                e.tensor_copy(o, i),
                                waits=(t_l, womm[p], woout[p], wosp[p]) + kw,
                                inc=jsem)
                    wosp[p] = r.do(jeng, lambda e, o=wslotb[:, 512:1024],
                                   i=slotf, h=wslotb[:, 0:512]:
                                   e.tensor_tensor(o, i, h, AL.subtract),
                                   waits=(t_js,), inc=jsem)
                    dma("sync",
                        d_wos[dg, ic // 2][:, (ic % 2) * 1024:(ic % 2) * 1024 + 512],
                        wslotb[:, 0:512], ("dgo0", "dgo1", "dcon")[p],
                        waits=(wosp[p],))
                    woout[p] = dma("sync",
                                   d_wos[dg, ic // 2][:, (ic % 2) * 1024 + 512:
                                                      (ic % 2) * 1024 + 1024],
                                   wslotb[:, 512:1024],
                                   ("dgo0", "dgo1", "dcon")[p], waits=())
                    t_w = wosp[p]
                    split_done["wo"] = (woout[0], woout[1], woout[2],
                                        wosp[0], wosp[1], wosp[2])
                else:
                    slotb = WOU.bitcast(BF16)[:, sl * 2048:(sl + 1) * 2048]
                    if ic % 2 == 0:
                        t_wo2 = dma("sync", slotb[:, 0:1024],
                                    d_wos[dg, ic // 2][:, 0:1024],
                                    "wo%d" % sl,
                                    waits=(wou_free[sl],) + split_done["wo"])
                        t_wo2b = dma("sync", slotb[:, 1024:2048],
                                     d_wos[dg, ic // 2][:, 1024:2048],
                                     "wo%d" % (sl + 2), waits=())
                    t_w = t_wo2 if ic % 2 == 0 else t_wo2b
                    wslotb = slotb[:, (ic % 2) * 1024:(ic % 2 + 1) * 1024]
                for tc in range(TT // 128):
                    bank = 4 + 2 * dg + tc
                    gh = LAMb[:, ic * 2 * TT + tc * 256: ic * 2 * TT + tc * 256 + 128]
                    gl = LAMb[:, ic * 2 * TT + tc * 256 + 128:
                              ic * 2 * TT + tc * 256 + 256]
                    w0 = ((t_w, drain.get(bank)) + tuple(t_g2c[tc])) \
                        if ic == 0 else (t_w,)
                    mm(pb[bank][:, 0:512], gh, wslotb[:, 0:512],
                       ic == 0, False, waits=w0)
                    mm(pb[bank][:, 0:512], gl, wslotb[:, 0:512], False, False)
                    gtok[2 * dg + tc] = mm(pb[bank][:, 0:512], gh,
                                           wslotb[:, 512:1024],
                                           False, ic == NIC - 1, inc="pe")
                wou_free[sl] = gtok[2 * dg + (TT // 128) - 1]
                t_jmm = wou_free[sl]
                womm[(dg * NIC + ic) % 3] = t_jmm
                if ic % 4 == 3 and dg == 0:
                    pump(1)

        if tile == 0:
            split_done["stage"] = split_done["stage"] + (wosp[1], woout[1])
            ksl[0] = ksl[1] = wosp[2]
            ksl[2] = ksl[3] = woout[2]

        def ln(r1s, outs, wv, wa, slot):
            c = slot * 5
            su, sq, mn, vr, rs_ = (STT[:, c + i:c + i + 1] for i in range(5))
            t_sm = r.do("vector", lambda e, i=r1s, o=su: e.reduce_sum(o, i, X),
                        waits=wv, inc="dv")
            t_sq = r.do("scalar", lambda e, i=r1s, o=outs, q=sq: e.activation(
                o, i, AF.Square, accum_out=q), waits=wv + wa, inc="ac")
            t_mn = r.do("vector", lambda e, i=su, o=mn: e.tensor_scalar_mul(
                o, i, 1.0 / D), waits=(t_sm,), inc="dv")
            t_vr = r.do("vector", lambda e, i=mn, o=vr: e.tensor_mul(o, i, i),
                        waits=(t_mn,), inc="dv")
            t_vr = r.do("vector", lambda e, q=sq, o=vr: e.scalar_tensor_tensor(
                o, q, 1.0 / D, o, AL.mult, AL.subtract),
                waits=(t_vr, t_sq), inc="dv")
            t_vr = r.do("vector", lambda e, o=vr: e.tensor_scalar_add(o, o, 1e-5),
                        waits=(t_vr,), inc="dv")
            t_rs2 = r.do("scalar", lambda e, i=vr, o=rs_: e.activation(o, i, AF.Sqrt),
                         waits=(t_vr,), inc="ac")
            t_rs2 = r.do("vector", lambda e, o=rs_: e.reciprocal(o, o),
                         waits=(t_rs2,), inc="dv")
            return r.do("vector", lambda e, i=r1s, o=outs, m=mn, s=rs_:
                        e.tensor_scalar(o, i, m, s, AL.subtract, AL.mult),
                        waits=(t_rs2,), inc="dv")

        t_y1 = [None] * (TT // 128)
        txq_l = [None] * (TT // 128)
        t_xq = [None] * (TT // 128)
        txq = None
        for tc in range(TT // 128):
            r1s = R1[:, tc * D:(tc + 1) * D]
            y1s = Y1[:, tc * D:(tc + 1) * D]
            t_r1 = None
            for dg in range(2):
                bank = 4 + 2 * dg + tc
                fill[bank] = gtok[2 * dg + tc]
                t_r1 = r.do("vector", lambda e,
                            o=R1[:, tc * D + dg * 512: tc * D + (dg + 1) * 512],
                            p=pb[bank][:, 0:512],
                            x=XTM[:, xto + tc * D + dg * 512:
                                   xto + tc * D + (dg + 1) * 512]:
                            e.tensor_add(o, p, x),
                            waits=(gtok[2 * dg + tc], t_xtm, t_r1) + split_done["stage"],
                            inc="dv")
                drain[bank] = t_r1
            t_y1[tc] = ln(r1s, y1s, (t_r1,), split_done["stage"], 2 * tc)
            scr = XFM[:, xfo + tc * D:xfo + (tc + 1) * D]
            tq = r.do("vector", lambda e, o=r1s, i=y1s: e.tensor_scalar(
                o, i, RSC[:, :], 127.0, AL.mult, AL.min),
                waits=(t_y1[tc], t_rs, txq), inc="dv")
            tq = r.do("vector", lambda e, o=r1s: e.tensor_scalar(
                o, o, -127.0, 0.5, AL.max, AL.add), waits=(tq,), inc="dv")
            tq = r.do("vector", lambda e, o=FFSI[:, :], i=r1s: e.tensor_copy(o, i),
                      waits=(tq,) + split_done["stage"], inc="dv")
            tq = r.do("vector", lambda e, o=scr, i=FFSI[:, :]: e.tensor_copy(o, i),
                      waits=(tq,), inc="dv")
            tq = r.do("vector", lambda e, o=FFS[:, :], i0=scr, i1=r1s:
                      e.tensor_tensor(o, i0, i1, AL.is_gt),
                      waits=(tq,), inc="dv")
            txq = r.do("vector", lambda e, o=r1s, i0=scr, i1=FFS[:, :]:
                       e.tensor_sub(o, i0, i1), waits=(tq,), inc="dv")
            txq_l[tc] = txq
            pump(2 if tc == 0 else 1)
            for k8 in range(NDC):
                bank = 2 + k8 % 2
                t_tr = r.do("tensor", lambda e, i=r1s[:, k8 * 128:(k8 + 1) * 128],
                            o=pb[bank][:, 0:128]:
                            e.transpose(o, i, IDE[:, :]),
                            waits=(txq, drain.get(bank)), inc="pe")
                fill[bank] = t_tr
                t_ev = r.do("scalar", lambda e, o=XNT[:, k8 * 128:(k8 + 1) * 128],
                            p=pb[bank][:, 0:128]: e.activation(o, p, AF.Copy),
                            waits=(t_tr, t_xnt_free), inc="ac")
                drain[bank] = t_ev
            t_xq[tc] = t_ev
            ffb = [None, None]
            for oh in range(2):
                bank = 4 + 2 * oh + tc
                for k8 in range(NDC):
                    tok = mm(pb[bank][:, 0:512],
                             XNT[:, k8 * 128:(k8 + 1) * 128],
                             WFF[:, k8 * D + oh * 512: k8 * D + (oh + 1) * 512],
                             k8 == 0, k8 == NDC - 1,
                             waits=(t_xq[tc], t_wffn, drain.get(bank))
                             if k8 == 0 else (),
                             inc="pe" if k8 == NDC - 1 else None)
                fill[bank] = tok
                ffb[oh] = tok
                t_xnt_free = tok
            ffn_tok[tc] = ffb

        tos = []
        for tc in range(TT // 128):
            r1s = R1[:, tc * D:(tc + 1) * D]
            y1s = Y1[:, tc * D:(tc + 1) * D]
            x2s = XTM[:, xto + tc * D:xto + (tc + 1) * D]
            t_f = None
            for oh in range(2):
                bank = 4 + 2 * oh + tc
                t_f = r.do("vector", lambda e, o=r1s[:, oh * 512:(oh + 1) * 512],
                           p=pb[bank][:, 0:512], bb=BFN[:, oh * 512:(oh + 1) * 512]:
                           e.scalar_tensor_tensor(o, p, SB_[:, :], bb,
                                                  AL.mult, AL.add),
                           waits=(ffn_tok[tc][oh], t_s3, t_f), inc="dv")
                drain[bank] = t_f
            t_fa = r.do("vector", lambda e, o=r1s, i=y1s:
                        e.tensor_add(o, o, i), waits=(t_f,), inc="dv")
            t_y2[tc] = ln(r1s, x2s, (t_fa,), (), 2 * tc + 1)
            tos.append(dma("scalar", d_out[t0 + tc * 128: t0 + (tc + 1) * 128, :],
                           x2s, "do", waits=(t_y2[tc],)))
        xtm_free[0] = tos[-1]
        xfm_free[par] = txq_l[-1]
        nout += len(tos)
        if s1n is not None:
            while s1n["ec"] < 2 * NIC:
                emit_s1_group(s1n)
            s1 = s1n

    r.do("gpsimd", lambda e: e.nop(), waits=(("do", 16 * nout),))

    sems = {}
    for s in ("pe", "dv", "ac", "po", "dcon", "dff", "dgl0", "dgl1",
              "dgv0", "dgv1", "dgo0", "dgo1",
              "dxf0", "dxf1", "dxt",
              "do", "dw0", "dw1", "dw2", "dw3",
              "dl0", "dl1", "dl2", "dl3", "dl4", "dl5", "dl6", "dl7",
              "wo0", "wo1", "wo2", "wo3"):
        sems[s] = nc.alloc_semaphore(s)

    with nc.Block() as blk:
        def mk(eng):
            def body(e):
                for fn, waits, inc, by in r.ops[eng]:
                    for s, v in waits:
                        e.wait_ge(sems[s], v)
                    ins = fn(e)
                    if inc is not None:
                        ins.then_inc(sems[inc], by)
            return body
        blk.tensor(mk("tensor"))
        blk.vector(mk("vector"))
        blk.scalar(mk("scalar"))
        blk.gpsimd(mk("gpsimd"))
        blk.sync(mk("sync"))

    handles = dict(XI=XI, ZS=ZS, VV_=VV_, LAM=LAM, GC=GC, KK=KK, XFM=XFM,
                   XTM=XTM, R1=R1, Y1=Y1, XNT=XNT, WFF=WFF, STATE=STATE,
                   WOM2=WOM2, S0=S0, UU=UU, GB=GB, SB_=SB_, RSC=RSC)
    return nc, handles


def host_layout(inputs, T=2048):
    x = np.asarray(inputs["L0_attn_x"], np.float32)
    Win = np.asarray(inputs["L0_attn_Win"], np.float32)
    Wl = np.asarray(inputs["L0_attn_Wl"], np.float32)
    bl = np.asarray(inputs["L0_attn_bl"], np.float32)
    Wv = np.asarray(inputs["L0_attn_Wv"], np.float32)
    bv = np.asarray(inputs["L0_attn_bv"], np.float32)
    Wout = np.asarray(inputs["L0_attn_Wout"], np.float32)
    om = np.asarray(inputs["L0_attn_omega"], np.float32).reshape(-1)
    tern = np.asarray(inputs["L0_ffn_ternary_weight"], np.int32)
    bias = np.asarray(inputs["L0_ffn_bias"], np.float32)
    sc = np.asarray(inputs["L0_ffn_act_scale"], np.float32).reshape(1, 1)

    w_in = np.ascontiguousarray(
        Win.reshape(32, 128, 8, 128).transpose(0, 3, 2, 1).reshape(32, 128, 1024))
    w_l = np.ascontiguousarray(
        Wl.reshape(16, 128, 16, 128).transpose(0, 3, 2, 1).reshape(16, 128, 2048))
    w_v = np.ascontiguousarray(
        Wv.reshape(16, 128, 16, 128).transpose(0, 3, 2, 1).reshape(16, 128, 2048))
    w_out = np.ascontiguousarray(
        Wout.reshape(2, 512, 16, 128).transpose(0, 2, 3, 1))
    w_ffn = np.ascontiguousarray(tern.T)
    bl_fm = np.ascontiguousarray(bl.reshape(16, 128).T)
    bv_fm = np.ascontiguousarray(bv.reshape(16, 128).T)
    om_fm = np.ascontiguousarray(om.reshape(16, 128).T)
    b_ffn = np.ascontiguousarray(np.broadcast_to(bias, (128, 1024)))

    maps = []
    for b in range(x.shape[0]):
        maps.append({
            "x_fm": np.ascontiguousarray(x[b, :T].T),
            "x_tm": np.ascontiguousarray(x[b, :T]),
            "w_in": w_in, "w_l": w_l, "w_v": w_v, "w_out": w_out,
            "w_ffn": w_ffn, "bl_fm": bl_fm, "bv_fm": bv_fm, "om_fm": om_fm,
            "b_ffn": b_ffn, "scal": sc,
        })
    return maps


def kernel(**inputs):
    maps = host_layout(inputs, T=2048)
    nc, _ = build(n_tiles=8)
    res = run_bass_kernel_spmd(nc, maps, core_ids=list(range(8)))
    out = np.stack([np.asarray(res.results[b]["out"], np.float32)
                    for b in range(8)], axis=0)
    return out
